# version 26
# baseline (speedup 1.0000x reference)
import numpy as np
import ml_dtypes
import concourse.bass as bass
import concourse.mybir as mybir
from concourse.bass_utils import run_bass_kernel_spmd

F32 = mybir.dt.float32
BF16 = mybir.dt.bfloat16
AF = mybir.ActivationFunctionType
ALU = mybir.AluOpType

E = 1024
NCH = 8
SEQ = 4096
NB = 4
NPASS = 4
OUT_T = 512
LP, RP = 32, 16
P = LP + OUT_T + RP
HALF = P // 2
XCOLS = 2048 + LP + 32
BLOCKS = [(0, 128), (128, 128), (256, 128), (384, 128), (512, 48)]
NBLK = len(BLOCKS)
NEGM = -8.0e30
MSLOT = 3


def blk_window(bi):
    r0, sz = BLOCKS[bi]
    ws = max(0, r0 - 128)
    we = min(P, r0 + sz + 10)
    pieces = []
    off = 0
    while ws + off < we:
        pieces.append((off, min(128, we - (ws + off))))
        off += 128
    return ws, pieces


ENGS = ("sync", "gpsimd", "tensor", "vector", "scalar")
DMA_ENGS = ("sync", "gpsimd")


class _St:
    __slots__ = ("eng", "emit", "deps", "idx", "seq", "dma", "note")


class Prog:
    def __init__(self):
        self.steps = []
        self.lw = {}
        self.rd = {}

    def step(self, eng, emit, r=(), w=(), dma=False, note=""):
        s = _St()
        s.eng, s.emit, s.dma, s.note = eng, emit, dma, note
        s.deps = set()
        for x in r:
            d = self.lw.get(x)
            if d is not None:
                s.deps.add(d)
        for x in w:
            d = self.lw.get(x)
            if d is not None:
                s.deps.add(d)
            for dd in self.rd.get(x, ()):
                s.deps.add(dd)
        s.idx = len(self.steps)
        self.steps.append(s)
        for x in r:
            self.rd.setdefault(x, []).append(s)
        for x in w:
            self.lw[x] = s
            self.rd[x] = []
        return s

    NLANE = 8

    def emit(self, nc, block):
        cnt = {e: 0 for e in ENGS}
        dcnt = {e: 0 for e in DMA_ENGS}
        for s in self.steps:
            if s.dma:
                j = dcnt[s.eng]
                dcnt[s.eng] += 1
                s.seq = (j % self.NLANE, j // self.NLANE + 1)
            else:
                cnt[s.eng] += 1
                s.seq = cnt[s.eng]
        sems = {e: nc.alloc_semaphore("sem_" + e) for e in ENGS if cnt[e]}
        dsems = {e: [nc.alloc_semaphore("dsem_%s%d" % (e, i)) for i in range(self.NLANE)]
                 for e in DMA_ENGS if dcnt[e]}
        by_eng = {e: [s for s in self.steps if s.eng == e] for e in ENGS}

        def body(eng_obj, ename):
            wm = {}
            for s in by_eng[ename]:
                need = {}
                for d in s.deps:
                    if d.dma:
                        lane, use = d.seq
                        sem, val = dsems[d.eng][lane], 16 * use
                    else:
                        sem, val = sems[d.eng], d.seq
                    k = id(sem)
                    if need.get(k, (None, -1))[1] < val:
                        need[k] = (sem, val)
                if s.dma:
                    lane, use = s.seq
                    if use > 1:
                        sem = dsems[ename][lane]
                        k = id(sem)
                        if need.get(k, (None, -1))[1] < 16 * (use - 1):
                            need[k] = (sem, 16 * (use - 1))
                for k in sorted(need, key=lambda q: -need[q][1]):
                    sem, val = need[k]
                    if wm.get(k, 0) < val:
                        eng_obj.wait_ge(sem, val)
                        wm[k] = val
                last = s.emit(eng_obj)
                if s.dma:
                    lane, use = s.seq
                    last.then_inc(dsems[ename][lane], 16)
                else:
                    last.then_inc(sems[ename], 1)
            if ename == "sync" and dcnt["sync"]:
                for lane in range(self.NLANE):
                    n_uses = (dcnt["sync"] - lane + self.NLANE - 1) // self.NLANE
                    if n_uses:
                        eng_obj.wait_ge(dsems["sync"][lane], 16 * n_uses)

        for ename in ENGS:
            if not by_eng[ename]:
                continue
            getattr(block, ename)(lambda eng, _e=ename: body(eng, _e))


WNAMES = ("wfr", "wfk", "wfv", "wq", "wak", "wav", "wo")


def build_nc(stop_after=None):
    nc = bass.Bass("TRN2", target_bir_lowering=False)
    pr = Prog()

    x_d = nc.dram_tensor("x", [E, XCOLS], F32, kind="ExternalInput").ap()
    w_d = {w: nc.dram_tensor(w, [E, E], F32, kind="ExternalInput").ap() for w in WNAMES}
    mur_d = nc.dram_tensor("mur", [128, NCH], F32, kind="ExternalInput").ap()
    muk_d = nc.dram_tensor("muk", [128, NCH], F32, kind="ExternalInput").ap()
    bq_d = nc.dram_tensor("bq", [128, NCH], F32, kind="ExternalInput").ap()
    bk_d = nc.dram_tensor("bk", [128, NCH], F32, kind="ExternalInput").ap()
    bo_d = nc.dram_tensor("bo", [128, NCH], F32, kind="ExternalInput").ap()
    bv_d = nc.dram_tensor("bv", [1, E], F32, kind="ExternalInput").ap()
    id_d = nc.dram_tensor("ident", [128, 128], BF16, kind="ExternalInput").ap()
    mk_d = nc.dram_tensor("masks", [NPASS * NBLK * MSLOT * 128, 128], BF16, kind="ExternalInput").ap()
    cm_d = nc.dram_tensor("cmask", [128, NPASS], F32, kind="ExternalInput").ap()
    y_d = nc.dram_tensor("y", [E, 2048], F32, kind="ExternalOutput").ap()
    sc_d = {w: nc.dram_tensor("sc_" + w, [128, 2 * NCH * E], BF16, kind="Internal").ap()
            for w in WNAMES}

    A = nc.alloc_sbuf_tensor
    X = A("X", [128, NCH * P], F32)
    T = A("T", [128, NCH * P], F32)
    bfAh = A("bfAh", [128, NCH * P], BF16)
    bfAl = A("bfAl", [128, NCH * P], BF16)
    bfBh = A("bfBh", [128, NCH * P], BF16)
    bfBl = A("bfBl", [128, NCH * P], BF16)
    vth = A("vth", [128, NBLK * E], BF16)
    vtl = A("vtl", [128, NBLK * E], BF16)
    qbh = A("qbh", [128, NCH * 256], BF16)
    qbl = A("qbl", [128, NCH * 256], BF16)
    aeh = A("aeh", [128, NCH * 256], BF16)
    ael = A("ael", [128, NCH * 256], BF16)
    e32x = [A("e32%d" % i, [128, 2 * MSLOT * 128], F32) for i in range(3)]
    elx = [A("el%d" % i, [128, 2 * MSLOT * 128], BF16) for i in range(3)]
    msb = [A("msb%d" % i, [128, MSLOT * 128], BF16) for i in range(2)]
    wsl = [A("wsl%d" % i, [128, 2 * NCH * E], BF16) for i in range(2)]
    dtmp2 = [A("dtmp0", [128, HALF], F32), A("dtmp1", [128, HALF], F32)]
    xtmp2 = [A("xtmp0", [128, HALF], F32), A("xtmp1", [128, HALF], F32)]
    katmp2 = [A("katmp0", [128, HALF], F32), A("katmp1", [128, HALF], F32)]
    pvt = A("pvt", [128, 128], F32)
    mb_sb = A("mb_sb", [128, P], F32)
    rb32 = A("rb32", [128, 128], F32)
    m32 = A("m32", [1, P], F32)
    v32 = A("v32", [1, P], F32)
    rs32x = [A("rs32a", [1, 256], F32), A("rs32b", [1, 256], F32)]
    lncol = A("lncol", [128, NCH], F32)
    mu_r = A("mu_r", [128, NCH], F32)
    mu_k = A("mu_k", [128, NCH], F32)
    dmu = A("dmu", [128, NCH], F32)
    bqs = A("bqs", [128, NCH], F32)
    bks = A("bks", [128, NCH], F32)
    bos = A("bos", [128, NCH], F32)
    bvh = A("bvh", [1, E], BF16)
    bvl = A("bvl", [1, E], BF16)
    idt = A("idt", [128, 128], BF16)
    cms = A("cms", [128, NPASS], F32)
    _vthA = nc.lookup_mloc(vth).addr
    _vtlA = nc.lookup_mloc(vtl).addr
    wstg = [nc.alloc_sbuf_tensor_at("wstg%d" % i, [128, 1024], F32, offset=_vthA + 4096 * i)
            for i in range(2)]
    whib = [nc.alloc_sbuf_tensor_at("whib%d" % i, [128, 1024], BF16, offset=_vtlA + 4096 + 4096 * i)
            for i in range(2)]
    wlob = [nc.alloc_sbuf_tensor_at("wlob0", [128, 1024], BF16, offset=_vtlA + 6144),
            nc.alloc_sbuf_tensor_at("wlob1", [128, 1024], BF16, offset=_vthA + 8192)]
    WSTG_R = [["wstg0", "vth0", "vth1"], ["wstg1", "vth2", "vth3"]]
    WHIB_R = [["whib0", "vtl2"], ["whib1", "vtl4"]]
    WLOB_R = [["wlob0", "vtl3"], ["wlob1", "vth4"]]
    bvs = nc.alloc_sbuf_tensor_at("bvs", [1, E], F32, offset=_vtlA)
    onec = A("onec", [1, 128], BF16)
    onecf = A("onecf", [1, 128], F32)
    onem = A("onem", [128, 1], BF16)
    onemf = A("onemf", [128, 1], F32)
    ones = A("ones", [128, 1], BF16)
    epsap = A("epsap", [1, 1], F32)
    nhalf = A("nhalf", [1, 1], F32)
    sc8 = A("sc8", [128, 1], F32)
    nb12 = A("nb12", [128, 1], F32)

    PA = nc.alloc_psum_tensor
    gp = [PA("gp0", [128, 512], F32), PA("gp1", [128, 512], F32)]
    sb = [PA("sb0", [128, 512], F32), PA("sb1", [128, 512], F32)]
    pv = [PA("pv0", [128, 512], F32), PA("pv1", [128, 512], F32)]
    misc = PA("misc", [128, 512], F32)
    bca = PA("bca", [128, 512], F32)

    def thi_chs(ci, a, b):
        return T.ap().bitcast(BF16).rearrange("p (c t two) -> p c t two", c=NCH, two=2)[:, ci, a:b, 1]

    def thi_cols(a, b):
        return T.ap().bitcast(BF16).rearrange("p (n two) -> p n two", two=2)[:, a:b, 1]

    def thi_sview(a, b):
        return T.ap().bitcast(BF16).rearrange("p (c t two) -> p c t two", c=NCH, two=2)[:, :, a:b, 1]

    def ehv(pb):
        return e32x[pb].ap().bitcast(BF16).rearrange("p (n two) -> p n two", two=2)[:, :, 1]

    _cap = [None]

    def st(eng, emit, r=(), w=(), dma=False, note=""):
        if _cap[0] is not None:
            _cap[0].append((eng, emit, r, w, dma, note))
            return None
        return pr.step(eng, emit, r=r, w=w, dma=dma, note=note)

    def capture(fn, *a, **k):
        q = []
        prev = _cap[0]
        _cap[0] = q
        try:
            fn(*a, **k)
        finally:
            _cap[0] = prev
        return q

    def replay(steps):
        for (eng, emit, r, w, dma, note) in steps:
            pr.step(eng, emit, r=r, w=w, dma=dma, note=note)

    wk_ctr = [0]

    def warmkeep(dep_res):
        wk_ctr[0] += 1
        st("tensor", lambda t: t.matmul(bca[0:1, 508:512], onec[0:1, 0:1], onec[0:1, 0:4],
                                        start=True, stop=True),
           r=list(dep_res), w=["bca"], note="warmkeep%d" % wk_ctr[0])

    def weave(chain, bg_groups, lead=2):
        nt = sum(1 for s in chain if s[0] == "tensor")
        per = max(1, (len(bg_groups) - lead) // max(1, nt + 1))
        gi = 0
        for j in range(min(lead, len(bg_groups))):
            replay(bg_groups[gi])
            gi += 1
        k = 0
        for s in chain:
            if s[0] == "tensor":
                for j in range(per):
                    if gi < len(bg_groups):
                        replay(bg_groups[gi])
                        gi += 1
            pr.step(s[0], s[1], r=s[2], w=s[3], dma=s[4], note=s[5])
        for j in range(gi, len(bg_groups)):
            replay(bg_groups[j])

    def weave_even(chain, bg_groups):
        nt = sum(1 for s in chain if s[0] == "tensor")
        if not bg_groups:
            replay(chain)
            return
        stride = max(1, nt // (len(bg_groups) + 1))
        gi = 0
        k = 0
        for s in chain:
            if s[0] == "tensor":
                k += 1
                if k % stride == 0 and gi < len(bg_groups):
                    replay(bg_groups[gi])
                    gi += 1
            pr.step(s[0], s[1], r=s[2], w=s[3], dma=s[4], note=s[5])
        for j in range(gi, len(bg_groups)):
            replay(bg_groups[j])

    def groups_of(fn, *a, **k):
        q = capture(fn, *a, **k)
        gs = []
        for s in q:
            if s[0] == "tensor" or not gs:
                gs.append([s])
            else:
                gs[-1].append(s)
        return gs

    def init_v(v):
        v.memset(onec[:, :], 1.0)
        v.memset(onecf[:, :], 1.0)
        v.memset(onem[:, :], 1.0 / 1024.0)
        v.memset(onemf[:, :], 1.0 / 1024.0)
        v.memset(ones[:, :], 1.0)
        v.memset(epsap[:, :], 1e-5)
        v.memset(nhalf[:, :], -0.5)
        v.memset(sc8[:, :], 0.125)
        v.memset(nb12[:, :], -12.0)
        for t in gp + sb + pv + [misc, bca]:
            v.memset(t[:, :], 1.0)
        v.memset(rs32x[0][:, :], 1.0)
        return v.memset(rs32x[1][:, :], 1.0)
    st("vector", init_v, w=["consts", "gp0", "gp1", "s0", "s1", "pv0", "pv1", "misc", "bca"], note="init")

    st("sync", lambda s: s.dma_start(out=idt[:, :], in_=id_d), w=["idt"], dma=True, note="ident")
    st("sync", lambda s: s.dma_start(out=mu_r[:, :], in_=mur_d), w=["mu"], dma=True)
    st("sync", lambda s: s.dma_start(out=mu_k[:, :], in_=muk_d), w=["mu"], dma=True)
    st("sync", lambda s: s.dma_start(out=bqs[:, :], in_=bq_d), w=["battn"], dma=True)
    st("sync", lambda s: s.dma_start(out=bks[:, :], in_=bk_d), w=["battn"], dma=True)
    st("sync", lambda s: s.dma_start(out=bos[:, :], in_=bo_d), w=["battn"], dma=True)
    st("sync", lambda s: s.dma_start(out=bvs[:, :], in_=bv_d), w=["bv32"], dma=True)
    st("sync", lambda s: s.dma_start(out=cms[:, :], in_=cm_d), w=["cms"], dma=True)
    st("scalar", lambda s: s.copy(out=bvh[:, :], in_=bvs[:, :]), r=["bv32"], w=["bvh"])
    st("vector", lambda v: v.tensor_tensor(out=dmu[:, :], in0=mu_r[:, :], in1=mu_k[:, :], op=ALU.subtract),
       r=["mu"], w=["mu"])
    st("vector", lambda v: v.tensor_tensor(out=bvl[:, :], in0=bvs[:, :], in1=bvh[:, :], op=ALU.subtract),
       r=["bv32", "bvh"], w=["bvl"])

    def wsplit_chunk(w, ci):
        wsb = wstg[ci % 2]
        hib, lob = whib[ci % 2], wlob[ci % 2]
        sres, hres_, lres = WSTG_R[ci % 2], WHIB_R[ci % 2], WLOB_R[ci % 2]
        st("sync", lambda s, _w=w, _ci=ci, _d=wsb: s.dma_start(
            out=_d[:, :], in_=w_d[_w][_ci * 128:(_ci + 1) * 128, :]),
           w=sres, dma=True, note="Wld %s c%d" % (w, ci))
        st("scalar", lambda s, _o=hib, _i=wsb: s.copy(out=_o[:, :], in_=_i[:, :]),
           r=sres, w=hres_, note="Wsp hi %s c%d" % (w, ci))
        st("vector", lambda v, _o=lob, _i=wsb, _h=hib:
           v.tensor_tensor(out=_o[:, :], in0=_i[:, :], in1=_h[:, :], op=ALU.subtract),
           r=sres + hres_, w=lres, note="Wsp lo %s c%d" % (w, ci))
        st("gpsimd", lambda s, _w=w, _h=hib, _ci=ci: s.dma_start(
            out=sc_d[_w][:, _ci * E:(_ci + 1) * E], in_=_h[:, :]),
           r=hres_, w=["sc_%s_%d" % (w, ci)], dma=True, note="Wst %s h c%d" % (w, ci))
        st("gpsimd", lambda s, _w=w, _l=lob, _ci=ci: s.dma_start(
            out=sc_d[_w][:, NCH * E + _ci * E:NCH * E + (_ci + 1) * E], in_=_l[:, :]),
           r=lres, w=["sc_%s_%d" % (w, 8 + ci)], dma=True, note="Wst %s l c%d" % (w, ci))

    phaseW_groups = []
    for w in WNAMES:
        for ci in range(NCH):
            phaseW_groups.append(capture(wsplit_chunk, w, ci))

    use_ctr = [0]

    def load_use(w):
        k = use_ctr[0]
        use_ctr[0] += 1
        slot = wsl[k % 2]
        sname = "wsl%d" % (k % 2)
        st("gpsimd", lambda s, _w=w, _slot=slot: s.dma_start(out=_slot[:, :], in_=sc_d[_w]),
           r=["sc_%s_%d" % (w, j) for j in range(16)],
           w=[sname + "_hi", sname + "_lo"], dma=True, note="use-ld " + w)
        return slot, [sname + "_hi", sname + "_lo"]

    def wap(slot, reg, ci, d0, dn):
        base = reg * NCH * E + ci * E + d0
        return slot[:, base:base + dn]

    def chs(buf, ci, a, b):
        return buf[:, ci * P + a:ci * P + b]

    def hres(base, a, b):
        out = []
        if a < HALF:
            out.append(base + "0")
        if b > HALF:
            out.append(base + "1")
        return out

    mmbank = [("gp0", gp[0]), ("gp1", gp[1]), ("s0", sb[0]), ("s1", sb[1])]
    mmctr = [0]

    def nxbank():
        i = mmctr[0] % 4
        mmctr[0] += 1
        return mmbank[i]

    def sview(buf, a, b):
        return buf.ap().rearrange("p (c t) -> p c t", c=NCH)[:, :, a:b]

    def emit_ln_half(src, sres_base, hh, passlbl, want_hl=False):
        lbl = "%s.h%d" % (passlbl, hh)
        a, b = hh * HALF, (hh + 1) * HALF
        rT = "T%d" % hh
        rAh, rAl = "bfAh%d" % hh, "bfAl%d" % hh
        rBh, rBl = "bfBh%d" % hh, "bfBl%d" % hh
        rsrc = sres_base + "%d" % hh

        def mmm(t):
            ins = None
            for ci in range(NCH):
                ins = t.matmul(misc[0:1, 0:HALF], onemf[:, :], chs(src, ci, a, b),
                               start=(ci == 0), stop=(ci == NCH - 1))
            return ins
        st("tensor", mmm, r=[rsrc, "consts"], w=["misc"], note=lbl + " mean mm")
        st("vector", lambda v: v.tensor_copy(out=m32[:, a:b], in_=misc[0:1, 0:HALF]),
           r=["misc"], w=["m32" + str(hh)], note=lbl + " mean ev")
        st("tensor", lambda t: t.matmul(bca[:, 0:HALF], onecf[0:1, :], m32[:, a:b], start=True, stop=True),
           r=["m32" + str(hh), "consts"], w=["bca"], note=lbl + " mean bc")
        st("vector", lambda v: v.tensor_copy(out=mb_sb[:, a:b], in_=bca[:, 0:HALF]),
           r=["bca"], w=["mb" + str(hh)], note=lbl + " mb ev")

        st("vector", lambda v: v.tensor_tensor(
            out=sview(T, a, b), in0=sview(src, a, b),
            in1=mb_sb.ap()[:, a:b].unsqueeze(1).broadcast_to([128, NCH, b - a]), op=ALU.subtract),
           r=[rsrc, "mb" + str(hh)], w=[rT], note=lbl + " tsub")

        for ci in range(NCH):
            sqb = dtmp2[hh] if ci % 2 == 0 else xtmp2[hh]
            rsb = ("dtmp%d" % hh) if ci % 2 == 0 else ("xtmp%d" % hh)
            st("scalar", lambda s, _ci=ci, _b=sqb: s.activation(out=_b[:, :], in_=chs(T, _ci, a, b),
                                                                func=AF.Square),
               r=[rT], w=[rsb], note=lbl + " sq%d" % ci)
            st("scalar", lambda s, _ci=ci, _b=sqb: s.copy(out=chs(bfBh, _ci, a, b), in_=_b[:, :]),
               r=[rsb], w=[rBh], note=lbl + " sqh%d" % ci)
            st("vector", lambda v, _ci=ci, _b=sqb: v.tensor_tensor(out=chs(bfBl, _ci, a, b), in0=_b[:, :],
                                                                   in1=chs(bfBh, _ci, a, b), op=ALU.subtract),
               r=[rsb, rBh], w=[rBl], note=lbl + " sql%d" % ci)

        def mmv(t):
            first = True
            ins = None
            for ci in range(NCH):
                for bb in (bfBh, bfBl):
                    ins = t.matmul(misc[32:33, 0:HALF], onem[:, :], chs(bb, ci, a, b),
                                   start=first, stop=(ci == NCH - 1 and bb is bfBl))
                    first = False
            return ins
        st("tensor", mmv, r=[rBh, rBl, "consts"], w=["misc"], note=lbl + " var mm")
        st("vector", lambda v: v.tensor_copy(out=v32[:, a:b], in_=misc[32:33, 0:HALF]),
           r=["misc"], w=["v32" + str(hh)], note=lbl + " var ev")
        st("scalar", lambda s: s.activation(out=v32[:, a:b], in_=v32[:, a:b], func=AF.Ln,
                                            bias=epsap[0:1, :], scale=1.0),
           r=["v32" + str(hh), "consts"], w=["v32" + str(hh)], note=lbl + " lnv")
        st("scalar", lambda s: s.activation(out=v32[:, a:b], in_=v32[:, a:b], func=AF.Exp,
                                            bias=0.0, scale=nhalf[0:1, :]),
           r=["v32" + str(hh), "consts"], w=["v32" + str(hh)], note=lbl + " expv")
        st("tensor", lambda t: t.matmul(bca[:, 0:HALF], onecf[0:1, :], v32[:, a:b], start=True, stop=True),
           r=["v32" + str(hh), "consts"], w=["bca"], note=lbl + " r bc")
        st("vector", lambda v: v.tensor_copy(out=mb_sb[:, a:b], in_=bca[:, 0:HALF]),
           r=["bca"], w=["mb" + str(hh)], note=lbl + " rb ev")

        st("vector", lambda v: v.tensor_tensor(
            out=sview(T, a, b), in0=sview(T, a, b),
            in1=mb_sb.ap()[:, a:b].unsqueeze(1).broadcast_to([128, NCH, b - a]), op=ALU.mult),
           r=["mb" + str(hh)], w=[rT], note=lbl + " ln mul")
        if hh == 0 and not want_hl:
            st("vector", lambda v: v.tensor_copy(out=lncol.ap().unsqueeze(2),
                                                 in_=sview(T, HALF - 1, HALF)),
               r=[rT], w=["lncol"], note=lbl + " lncol")
        if want_hl:
            st("vector", lambda v: v.tensor_tensor(out=sview(bfAl, a, b), in0=sview(T, a, b),
                                                   in1=thi_sview(a, b), op=ALU.subtract),
               r=[rT], w=[rAl], note=lbl + " ln lo")

    def emit_mixes_half(hh, passlbl):
        lbl = "%s.h%d" % (passlbl, hh)
        a, b = hh * HALF, (hh + 1) * HALF
        a1 = a + 1
        rdeps = ["T%d" % hh] + (["lncol"] if hh == 1 else [])
        rAh, rAl = "bfAh%d" % hh, "bfAl%d" % hh
        rBh, rBl = "bfBh%d" % hh, "bfBl%d" % hh
        for ci in range(NCH):
            dtm = dtmp2[hh] if ci % 2 == 0 else xtmp2[hh]
            rd = ("dtmp%d" % hh) if ci % 2 == 0 else ("xtmp%d" % hh)
            xtm = katmp2[hh] if ci % 2 == 0 else dtmp2[1 - hh]
            rx = ("katmp%d" % hh) if ci % 2 == 0 else ("dtmp%d" % (1 - hh))
            def dd(v, _ci=ci, _dtm=dtm):
                ins = v.tensor_tensor(out=_dtm[:, 1:b - a], in0=chs(T, _ci, a1, b),
                                      in1=chs(T, _ci, a1 - 1, b - 1), op=ALU.subtract)
                if hh == 0:
                    ins = v.memset(_dtm[:, 0:1], 0.0)
                else:
                    ins = v.tensor_tensor(out=_dtm[:, 0:1], in0=chs(T, _ci, a, a + 1),
                                          in1=lncol[:, _ci:_ci + 1], op=ALU.subtract)
                return ins
            st("vector", dd, r=rdeps, w=[rd], note=lbl + " d%d" % ci)
            def xk(v, _ci=ci, _dtm=dtm, _xtm=xtm):
                ins = v.scalar_tensor_tensor(out=_xtm[:, 1:b - a], in0=_dtm[:, 1:b - a],
                                             scalar=mu_k[:, _ci:_ci + 1], in1=chs(T, _ci, a1 - 1, b - 1),
                                             op0=ALU.mult, op1=ALU.add)
                if hh == 0:
                    ins = v.memset(_xtm[:, 0:1], 0.0)
                else:
                    ins = v.scalar_tensor_tensor(out=_xtm[:, 0:1], in0=_dtm[:, 0:1],
                                                 scalar=mu_k[:, _ci:_ci + 1], in1=lncol[:, _ci:_ci + 1],
                                                 op0=ALU.mult, op1=ALU.add)
                return ins
            st("vector", xk, r=[rd, "mu"] + rdeps, w=[rx], note=lbl + " xk%d" % ci)
            st("scalar", lambda s, _ci=ci, _xtm=xtm: s.copy(out=chs(bfBh, _ci, a, b), in_=_xtm[:, :]),
               r=[rx], w=[rBh], note=lbl + " xkh%d" % ci)
            st("gpsimd", lambda g, _ci=ci, _xtm=xtm: g.tensor_tensor(out=chs(bfBl, _ci, a, b), in0=_xtm[:, :],
                                                                      in1=chs(bfBh, _ci, a, b), op=ALU.subtract),
               r=[rx, rBh], w=[rBl], note=lbl + " xkl%d" % ci)
            def xr(v, _ci=ci, _dtm=dtm, _xtm=xtm):
                return v.scalar_tensor_tensor(out=_dtm[:, 0:b - a], in0=_dtm[:, 0:b - a],
                                              scalar=dmu[:, _ci:_ci + 1], in1=_xtm[:, 0:b - a],
                                              op0=ALU.mult, op1=ALU.add)
            st("vector", xr, r=[rd, rx, "mu"], w=[rd], note=lbl + " xr%d" % ci)
            st("scalar", lambda s, _ci=ci, _dtm=dtm: s.copy(out=chs(bfAh, _ci, a, b), in_=_dtm[:, :]),
               r=[rd], w=[rAh], note=lbl + " xrh%d" % ci)
            st("gpsimd", lambda g, _ci=ci, _dtm=dtm: g.tensor_tensor(out=chs(bfAl, _ci, a, b), in0=_dtm[:, :],
                                                                      in1=chs(bfAh, _ci, a, b), op=ALU.subtract),
               r=[rd, rAh], w=[rAl], note=lbl + " xrl%d" % ci)

    def emit_mm24(t, out_ap, slot, d0, dn, rhsh, rhsl):
        first = True
        ins = None
        for ci in range(NCH):
            combos = ((0, rhsh(ci)), (1, rhsh(ci)), (0, rhsl(ci)))
            for j, (reg, rhs) in enumerate(combos):
                ins = t.matmul(out_ap, wap(slot, reg, ci, d0, dn), rhs,
                               start=first, stop=(ci == NCH - 1 and j == 2))
                first = False
        return ins

    def ffn_proj_half(w_use, hh, kind, passlbl):
        slot, sres = w_use
        lbl = "%s.%s.h%d" % (passlbl, kind, hh)
        a, b = hh * HALF, (hh + 1) * HALF
        src_h, src_l = (bfBh, bfBl) if kind == "k" else (bfAh, bfAl)
        rsh_, rsl_ = (("bfBh%d" % hh, "bfBl%d" % hh) if kind == "k"
                      else ("bfAh%d" % hh, "bfAl%d" % hh))
        ktm = katmp2[hh]
        rkt = "katmp%d" % hh
        for dj in range(NCH):
            bres, bank = nxbank()
            def mm(t, _dj=dj, _bank=bank, _slot=slot):
                return emit_mm24(t, _bank[:, 0:HALF], _slot, _dj * 128, 128,
                                 lambda ci: chs(src_h, ci, a, b),
                                 lambda ci: chs(src_l, ci, a, b))
            st("tensor", mm, r=[rsh_, rsl_] + sres, w=[bres], note=lbl + " mm%d" % dj)
            if kind == "r":
                st("scalar", lambda s, _dj=dj, _bank=bank:
                   s.activation(out=chs(T, _dj, a, b), in_=_bank[:, 0:HALF], func=AF.Sigmoid),
                   r=[bres], w=["T%d" % hh], note=lbl + " sig%d" % dj)
            elif kind == "k":
                st("scalar", lambda s, _bank=bank: s.activation(out=ktm[:, :], in_=_bank[:, 0:HALF],
                                                                func=AF.Relu),
                   r=[bres], w=[rkt], note=lbl + " krel%d" % dj)
                st("scalar", lambda s: s.activation(out=ktm[:, :], in_=ktm[:, :], func=AF.Square),
                   r=[rkt], w=[rkt], note=lbl + " ksq%d" % dj)
                st("scalar", lambda s, _dj=dj: s.copy(out=chs(bfAh, _dj, a, b), in_=ktm[:, :]),
                   r=[rkt], w=["bfAh%d" % hh], note=lbl + " kh%d" % dj)
                st("vector", lambda v, _dj=dj: v.tensor_tensor(out=chs(bfAl, _dj, a, b), in0=ktm[:, :],
                                                               in1=chs(bfAh, _dj, a, b), op=ALU.subtract),
                   r=[rkt, "bfAh%d" % hh], w=["bfAl%d" % hh], note=lbl + " kl%d" % dj)
            else:
                st("vector", lambda v, _dj=dj, _bank=bank:
                   v.scalar_tensor_tensor(out=ktm[:, :], in0=_bank[:, 0:HALF], scalar=0.5,
                                          in1=chs(T, _dj, a, b), op0=ALU.mult, op1=ALU.mult),
                   r=[bres, "T%d" % hh], w=[rkt], note=lbl + " vsc%d" % dj)
                st("vector", lambda v, _dj=dj:
                   v.tensor_tensor(out=chs(X, _dj, a, b), in0=chs(X, _dj, a, b),
                                   in1=ktm[:, :], op=ALU.add),
                   r=[rkt], w=["X%d" % hh], note=lbl + " xadd%d" % dj)

    def attn_K_half(w_use, hh, passlbl):
        slot, sres = w_use
        lbl = "%s.K.h%d" % (passlbl, hh)
        a, b = hh * HALF, (hh + 1) * HALF
        for dj in range(NCH):
            bres, bank = nxbank()
            def mm(t, _dj=dj, _bank=bank, _slot=slot):
                return emit_mm24(t, _bank[:, 0:HALF], _slot, _dj * 128, 128,
                                 lambda ci: thi_chs(ci, a, b),
                                 lambda ci: chs(bfAl, ci, a, b))
            st("tensor", mm, r=["T%d" % hh, "bfAl%d" % hh] + sres, w=[bres], note=lbl + " mm%d" % dj)
            st("scalar", lambda s, _dj=dj, _bank=bank:
               s.add(out=chs(bfBh, _dj, a, b), in_=_bank[:, 0:HALF], add=bks[:, _dj:_dj + 1]),
               r=[bres, "battn"], w=["bfBh%d" % hh], note=lbl + " kh%d" % dj)
            st("vector", lambda v, _dj=dj, _bank=bank:
               v.scalar_tensor_tensor(out=chs(bfBl, _dj, a, b), in0=_bank[:, 0:HALF],
                                      scalar=bks[:, _dj:_dj + 1], in1=chs(bfBh, _dj, a, b),
                                      op0=ALU.add, op1=ALU.subtract),
               r=[bres, "battn", "bfBh%d" % hh], w=["bfBl%d" % hh], note=lbl + " kl%d" % dj)

    def attn_V_tile(w_use, ti, passlbl):
        slot, sres = w_use
        t0, tsz = BLOCKS[ti]
        lbl = "%s.V.t%d" % (passlbl, ti)
        lnres = hres("T", t0, t0 + tsz) + hres("bfAl", t0, t0 + tsz)
        for dh in range(2):
            bres, bank = nxbank()
            def mm(t, _t0=t0, _tsz=tsz, _dh=dh, _bank=bank, _slot=slot):
                first = True
                for ci in range(NCH):
                    for reg, hi in ((0, True), (1, True), (0, False)):
                        lhs = (thi_cols(ci * P + _t0, ci * P + _t0 + _tsz) if hi
                               else bfAl[:, ci * P + _t0:ci * P + _t0 + _tsz])
                        t.matmul(_bank[0:_tsz, 0:512], lhs,
                                 wap(_slot, reg, ci, _dh * 512, 512), start=first, stop=False)
                        first = False
                t.matmul(_bank[0:_tsz, 0:512], onec[0:1, 0:_tsz], bvh[:, _dh * 512:(_dh + 1) * 512],
                         start=False, stop=False)
                return t.matmul(_bank[0:_tsz, 0:512], onec[0:1, 0:_tsz], bvl[:, _dh * 512:(_dh + 1) * 512],
                                start=False, stop=True)
            st("tensor", mm, r=lnres + ["bvh", "bvl", "consts"] + sres, w=[bres], note=lbl + " mm%d" % dh)
            st("scalar", lambda s, _ti=ti, _tsz=tsz, _dh=dh, _bank=bank:
               s.copy(out=vth[0:_tsz, _ti * E + _dh * 512:_ti * E + (_dh + 1) * 512],
                      in_=_bank[0:_tsz, 0:512]),
               r=[bres], w=["vth%d" % ti], note=lbl + " vh%d" % dh)
            st("vector", lambda v, _ti=ti, _tsz=tsz, _dh=dh, _bank=bank:
               v.tensor_tensor(out=vtl[0:_tsz, _ti * E + _dh * 512:_ti * E + (_dh + 1) * 512],
                               in0=_bank[0:_tsz, 0:512],
                               in1=vth[0:_tsz, _ti * E + _dh * 512:_ti * E + (_dh + 1) * 512],
                               op=ALU.subtract),
               r=[bres, "vth%d" % ti], w=["vtl%d" % ti], note=lbl + " vl%d" % dh)

    def attn_Q_pair(qslot, qsres, bi0, nb2, passlbl):
        lbl = "%s.Qp%d" % (passlbl, bi0)
        r0 = BLOCKS[bi0][0]
        n = sum(BLOCKS[bi0 + j][1] for j in range(nb2))
        lnres = hres("T", r0, r0 + n) + hres("bfAl", r0, r0 + n)
        for dj in range(NCH):
            g = dj % 2
            def mm(t, _dj=dj, _g=g, _r0=r0, _n=n, _slot=qslot):
                return emit_mm24(t, gp[_g][:, 0:_n], _slot, _dj * 128, 128,
                                 lambda ci: thi_cols(ci * P + _r0, ci * P + _r0 + _n),
                                 lambda ci: bfAl[:, ci * P + _r0:ci * P + _r0 + _n])
            st("tensor", mm, r=lnres + qsres, w=["gp%d" % g], note=lbl + " Q%d" % dj)
            st("scalar", lambda s, _dj=dj, _g=g, _n=n:
               s.add(out=qbh[:, _dj * 256:_dj * 256 + _n], in_=gp[_g][:, 0:_n], add=bqs[:, _dj:_dj + 1]),
               r=["gp%d" % g, "battn"], w=["qbh"], note=lbl + " qh%d" % dj)
            st("vector", lambda v, _dj=dj, _g=g, _n=n:
               v.scalar_tensor_tensor(out=qbl[:, _dj * 256:_dj * 256 + _n], in0=gp[_g][:, 0:_n],
                                      scalar=bqs[:, _dj:_dj + 1], in1=qbh[:, _dj * 256:_dj * 256 + _n],
                                      op0=ALU.add, op1=ALU.subtract),
               r=["gp%d" % g, "battn", "qbh"], w=["qbl"], note=lbl + " ql%d" % dj)

    def attn_pairs(q, bi, qoff, qslot, qsres, passlbl):
        lbl = "%s.b%d" % (passlbl, bi)
        r0, sz = BLOCKS[bi]
        ws, pieces = blk_window(bi)
        npc = len(pieces) * 128
        mbuf = msb[bi % 2]
        mres = "msb%d" % (bi % 2)
        lnres = hres("bfAh", r0, r0 + sz) + hres("bfAl", r0, r0 + sz)
        kres = hres("bfBh", ws, ws + npc) + hres("bfBl", ws, ws + npc)

        def mld(s, _q=q, _bi=bi, _mb=mbuf):
            off = ((_q * NBLK) + _bi) * MSLOT * 128
            return s.dma_start(out=_mb.ap().rearrange("p (c s) -> p c s", c=MSLOT),
                               in_=mk_d[off:off + MSLOT * 128, :].rearrange("(c p) s -> p c s", p=128))
        st("sync", mld, w=[mres], dma=True, note=lbl + " mask")

        def head_stage(pr):
            pb = pr % 3
            e32b, elb = e32x[pb], elx[pb]
            for par in range(2):
                def mm(t, _pr=pr, _par=par):
                    p0 = 64 * _par
                    for p, (off, tp) in enumerate(pieces):
                        t.matmul(sb[_par][0:128, p * 128:(p + 1) * 128], idt[:, :],
                                 mbuf[:, p * 128:(p + 1) * 128], start=(p == 0), stop=False)
                        for (kb, qb) in ((bfBh, qbh), (bfBh, qbl), (bfBl, qbh)):
                            t.matmul(sb[_par][0:tp, p * 128:p * 128 + sz],
                                     kb[p0:p0 + 64, _pr * P + ws + off:_pr * P + ws + off + tp],
                                     qb[p0:p0 + 64, _pr * 256 + qoff:_pr * 256 + qoff + sz],
                                     start=False, stop=False, tile_position=(p0, 0))
                    return t.matmul(sb[_par][0:128, 0:128], idt[:, :], mbuf[:, 0:128],
                                    start=False, stop=True)
                st("tensor", mm, r=kres + ["qbh", "qbl", mres, "idt"], w=["s%d" % par],
                   note=lbl + " sc p%d.%d" % (pr, par))
            def ex(s, _pb=pb):
                s.activation(out=e32b[:, 0:npc], in_=sb[0][:, 0:npc],
                             func=AF.Exp, bias=nb12[:, :], scale=sc8[:, :])
                return s.activation(out=e32b[:, 384:384 + npc], in_=sb[1][:, 0:npc],
                                    func=AF.Exp, bias=nb12[:, :], scale=sc8[:, :])
            st("scalar", ex, r=["s0", "s1", "consts"], w=["e32_%d" % pb],
               note=lbl + " exp p%d" % pr)
            def el_(v, _pb=pb):
                v.tensor_tensor(out=elb[:, 0:npc], in0=e32b[:, 0:npc],
                                in1=ehv(_pb)[:, 0:npc], op=ALU.subtract)
                return v.tensor_tensor(out=elb[:, 384:384 + npc], in0=e32b[:, 384:384 + npc],
                                       in1=ehv(_pb)[:, 384:384 + npc], op=ALU.subtract)
            st("vector", el_, r=["e32_%d" % pb], w=["el_%d" % pb], note=lbl + " el p%d" % pr)

        for pr in range(NCH):
            head_stage(pr)
            if pr > 0:
                tail_a(pr - 1, lbl, sz, ws, pieces)
            if pr > 1:
                tail_b(pr - 2, lbl, sz, ws, pieces, bi)
        tail_a(NCH - 1, lbl, sz, ws, pieces)
        tail_b(NCH - 2, lbl, sz, ws, pieces, bi)
        tail_b(NCH - 1, lbl, sz, ws, pieces, bi)

    def attn_wo(q, bi0, nb2, oslot, osres, passlbl):
        lbl = "%s.b%d+%d" % (passlbl, bi0, nb2)
        r0 = BLOCKS[bi0][0]
        n = sum(BLOCKS[bi0 + j][1] for j in range(nb2))
        ares = ["aeh%d" % ((bi0 + j) % 2) for j in range(nb2)]
        ares += ["ael%d" % ((bi0 + j) % 2) for j in range(nb2)]
        for dj in range(NCH):
            g = dj % 2
            def mm(t, _dj=dj, _g=g, _n=n, _oslot=oslot):
                return emit_mm24(t, gp[_g][:, 0:_n], _oslot, _dj * 128, 128,
                                 lambda ci: aeh[:, ci * 256:ci * 256 + _n],
                                 lambda ci: ael[:, ci * 256:ci * 256 + _n])
            st("tensor", mm, r=ares + osres, w=["gp%d" % g], note=lbl + " Wo%d" % dj)
            st("vector", lambda v, _dj=dj, _g=g, _r0=r0, _n=n:
               v.scalar_tensor_tensor(out=X[:, _dj * P + _r0:_dj * P + _r0 + _n],
                                      in0=gp[_g][:, 0:_n], scalar=bos[:, _dj:_dj + 1],
                                      in1=X[:, _dj * P + _r0:_dj * P + _r0 + _n],
                                      op0=ALU.add, op1=ALU.add),
               r=["gp%d" % g, "battn"], w=hres("X", r0, r0 + n), note=lbl + " Woev%d" % dj)

    def tail_a(pr, lbl, sz, ws, pieces):
        pb = pr % 3
        for par in range(2):
            def mm(t, _par=par, _pb=pb, _sz=sz, _pieces=pieces):
                first = True
                ins = None
                r0p = 32 * _par
                for p, (off, tp) in enumerate(_pieces):
                    for hi in (True, False):
                        ee = ehv(_pb) if hi else elx[_pb]
                        ins = t.matmul(misc[r0p:r0p + 1, 384:384 + _sz], ones[0:tp, 0:1],
                                       ee[0:tp, _par * 384 + p * 128:_par * 384 + p * 128 + _sz],
                                       start=first, stop=(p == len(_pieces) - 1 and not hi))
                        first = False
                return ins
            st("tensor", mm, r=["e32_%d" % pb, "el_%d" % pb, "consts"], w=["misc"],
               note=lbl + " sums p%d.%d" % (pr, par))
        rsb = rs32x[pr % 2]
        def rsev(v, _sz=sz, _rsb=rsb):
            v.reciprocal(out=_rsb[:, 0:_sz], in_=misc[0:1, 384:384 + _sz])
            return v.reciprocal(out=_rsb[:, 128:128 + _sz], in_=misc[32:33, 384:384 + _sz])
        st("vector", rsev, r=["misc"], w=["rs32_%d" % (pr % 2)], note=lbl + " rs p%d" % pr)

    def tail_b(pr, lbl, sz, ws, pieces, bi):
        pb = pr % 3
        rsb = rs32x[pr % 2]
        def bcr(t, _sz=sz, _rsb=rsb):
            t.matmul(bca[0:64, 0:_sz], onecf[0:1, 0:64], _rsb[:, 0:_sz], start=True, stop=True)
            return t.matmul(bca[64:128, 0:_sz], onecf[0:1, 0:64], _rsb[:, 128:128 + _sz],
                            start=True, stop=True, tile_position=(0, 64))
        st("tensor", bcr, r=["rs32_%d" % (pr % 2), "consts"], w=["bca"], note=lbl + " rbc p%d" % pr)
        st("vector", lambda v, _sz=sz: v.tensor_copy(out=rb32[:, 0:_sz], in_=bca[:, 0:_sz]),
           r=["bca"], w=["rb32"], note=lbl + " rbe p%d" % pr)
        bk_i = pr % 2
        def mmpv(t, _pr=pr, _bk=bk_i, _pb=pb, _sz=sz, _ws=ws, _pieces=pieces):
            ins = None
            for par in range(2):
                hq = 2 * _pr + par
                firstmm = True
                for p, (off, tp) in enumerate(_pieces):
                    ti = (_ws + off) // 128
                    for j, (vb, hi) in enumerate(((vth, True), (vth, False), (vtl, True))):
                        ee = ehv(_pb) if hi else elx[_pb]
                        ins = t.matmul(
                            pv[_bk][64 * par:64 * par + 64, 0:_sz],
                            vb[0:tp, ti * E + 64 * hq:ti * E + 64 * hq + 64],
                            ee[0:tp, par * 384 + p * 128:par * 384 + p * 128 + _sz],
                            start=firstmm, stop=(p == len(_pieces) - 1 and j == 2),
                            tile_position=(0, 64 * par))
                        firstmm = False
            return ins
        vres = []
        for p, (off, tp) in enumerate(pieces):
            ti = (ws + off) // 128
            vres += ["vth%d" % ti, "vtl%d" % ti]
        st("tensor", mmpv, r=vres + ["e32_%d" % pb, "el_%d" % pb], w=["pv%d" % bk_i],
           note=lbl + " PV p%d" % pr)
        st("vector", lambda v, _sz=sz, _bk=bk_i:
           v.tensor_tensor(out=pvt[:, 0:_sz], in0=pv[_bk][:, 0:_sz], in1=rb32[:, 0:_sz], op=ALU.mult),
           r=["pv%d" % bk_i, "rb32"], w=["pvt"], note=lbl + " pvm p%d" % pr)
        po = pr * 256 + (bi % 2) * 128
        st("scalar", lambda s, _po=po, _sz=sz: s.copy(out=aeh[:, _po:_po + _sz],
                                                      in_=pvt[:, 0:_sz]),
           r=["pvt"], w=["aeh%d" % (bi % 2)], note=lbl + " ah p%d" % pr)
        st("vector", lambda v, _po=po, _sz=sz:
           v.tensor_tensor(out=ael[:, _po:_po + _sz], in0=pvt[:, 0:_sz],
                           in1=aeh[:, _po:_po + _sz], op=ALU.subtract),
           r=["pvt", "aeh%d" % (bi % 2)], w=["ael%d" % (bi % 2)], note=lbl + " al p%d" % pr)

    def _par_off(par):
        return par * 384

    def xldh(q, hh):
        a, b = (0, HALF) if hh == 0 else (HALF, P)
        def f(s, _q=q, _a=a, _b=b):
            return s.dma_start(out=X.ap().rearrange("p (c t) -> p c t", c=NCH)[:, :, _a:_b],
                               in_=x_d[:, _q * OUT_T + _a:_q * OUT_T + _b].rearrange("(c p) t -> p c t", p=128))
        st("sync", f, w=["X%d" % hh], dma=True, note="q%d xld%d" % (q, hh))

    def ysth(q, hh):
        a, b = (LP, HALF) if hh == 0 else (HALF, LP + OUT_T)
        def f(s, _q=q, _a=a, _b=b):
            return s.dma_start(
                out=y_d[:, _q * OUT_T + _a - LP:_q * OUT_T + _b - LP].rearrange("(c p) t -> p c t", p=128),
                in_=X.ap().rearrange("p (c t) -> p c t", c=NCH)[:, :, _a:_b])
        st("sync", f, r=["X%d" % hh], dma=True, note="q%d yst%d" % (q, hh))

    def emit_cmask(q):
        st("vector", lambda v, _q=q: v.tensor_scalar(
            out=X.ap().rearrange("p (c t) -> p c t", c=NCH)[:, :, LP - 1:LP],
            in0=X.ap().rearrange("p (c t) -> p c t", c=NCH)[:, :, LP - 1:LP],
            scalar1=cms[:, _q:_q + 1], scalar2=None, op0=ALU.mult),
           r=["cms"], w=["X0"], note="q%d cmask" % q)

    def lnmix(hh, phase, want_hl=False):
        emit_ln_half(X, "X", hh, phase + " ln", want_hl=want_hl)
        if not want_hl:
            emit_mixes_half(hh, phase)

    if stop_after is not None:
        for g in phaseW_groups:
            replay(g)
        for q in range(NPASS):
            lbl = "q%d" % q
            xldh(q, 0)
            xldh(q, 1)
            f1 = lbl + " f1"
            lnmix(0, f1)
            lnmix(1, f1)
            if stop_after != "mix1":
                wr = load_use("wfr")
                ffn_proj_half(wr, 0, "r", f1)
                ffn_proj_half(wr, 1, "r", f1)
                wk = load_use("wfk")
                ffn_proj_half(wk, 0, "k", f1)
                ffn_proj_half(wk, 1, "k", f1)
                wv = load_use("wfv")
                ffn_proj_half(wv, 0, "v", f1)
                ffn_proj_half(wv, 1, "v", f1)
                if stop_after != "ffn1":
                    at = lbl + " at"
                    lnmix(0, at + " ln2", want_hl=True)
                    lnmix(1, at + " ln2", want_hl=True)
                    if stop_after != "ln2":
                        wak_u = load_use("wak")
                        attn_K_half(wak_u, 0, at)
                        attn_K_half(wak_u, 1, at)
                        wav_u = load_use("wav")
                        for ti in range(NBLK):
                            attn_V_tile(wav_u, ti, at)
                        qslot, qsres = load_use("wq")
                        oslot, osres = load_use("wo")
                        for bi0, nb2 in ((0, 2), (2, 2), (4, 1)):
                            attn_Q_pair(qslot, qsres, bi0, nb2, at)
                            for j in range(nb2):
                                attn_pairs(q, bi0 + j, 128 * j, qslot, qsres, at)
                            attn_wo(q, bi0, nb2, oslot, osres, at)
            ysth(q, 0)
            ysth(q, 1)
    else:
        prev_kv = None
        for q in range(NPASS):
            lbl = "q%d" % q
            f1 = lbl + " f1"
            xldh(q, 0)
            ch0 = capture(lnmix, 0, f1)
            if prev_kv is not None:
                weave(ch0, prev_kv)
                ysth(q - 1, 1)
            else:
                weave(ch0, phaseW_groups, lead=8)
            xldh(q, 1)
            wr = load_use("wfr")
            wk = load_use("wfk")
            weave(capture(lnmix, 1, f1),
                  groups_of(ffn_proj_half, wr, 0, "r", f1) + groups_of(ffn_proj_half, wk, 0, "k", f1))
            ffn_proj_half(wr, 1, "r", f1)
            wv = load_use("wfv")
            ffn_proj_half(wv, 0, "v", f1)
            at = lbl + " at"
            weave(capture(lnmix, 0, at + " ln2", True),
                  groups_of(ffn_proj_half, wk, 1, "k", f1) + groups_of(ffn_proj_half, wv, 1, "v", f1))
            wak_u = load_use("wak")
            wav_u = load_use("wav")
            weave(capture(lnmix, 1, at + " ln2", True),
                  groups_of(attn_K_half, wak_u, 0, at)
                  + groups_of(attn_V_tile, wav_u, 0, at) + groups_of(attn_V_tile, wav_u, 1, at))
            attn_K_half(wak_u, 1, at)
            attn_V_tile(wav_u, 2, at)
            qslot, qsres = load_use("wq")
            attn_Q_pair(qslot, qsres, 0, 2, at)
            weave_even(capture(attn_pairs, q, 0, 0, qslot, qsres, at),
                       groups_of(attn_V_tile, wav_u, 3, at)
                       + groups_of(attn_V_tile, wav_u, 4, at))
            oslot, osres = load_use("wo")
            attn_pairs(q, 1, 128, qslot, qsres, at)
            attn_wo(q, 0, 2, oslot, osres, at)
            attn_Q_pair(qslot, qsres, 2, 2, at)
            attn_pairs(q, 2, 0, qslot, qsres, at)
            attn_pairs(q, 3, 128, qslot, qsres, at)
            f2 = lbl + " f2"
            def cm_ln3(_q=q):
                emit_cmask(_q)
                lnmix(0, f2)
            attn_wo(q, 2, 2, oslot, osres, at)
            bgb = (groups_of(attn_Q_pair, qslot, qsres, 4, 1, at)
                   + groups_of(attn_pairs, q, 4, 0, qslot, qsres, at)
                   + groups_of(attn_wo, q, 4, 1, oslot, osres, at))
            weave(capture(cm_ln3), bgb, lead=12)
            wr = load_use("wfr")
            wk = load_use("wfk")
            weave(capture(lnmix, 1, f2),
                  groups_of(ffn_proj_half, wr, 0, "r", f2) + groups_of(ffn_proj_half, wk, 0, "k", f2))
            ffn_proj_half(wr, 1, "r", f2)
            wv = load_use("wfv")
            ffn_proj_half(wv, 0, "v", f2)
            ysth(q, 0)
            prev_kv = (groups_of(ffn_proj_half, wk, 1, "k", f2)
                       + groups_of(ffn_proj_half, wv, 1, "v", f2))
        for g in prev_kv:
            replay(g)
        ysth(NPASS - 1, 1)

    with nc.Block() as block:
        pr.emit(nc, block)
    return nc


def _stage_masks(h):
    out = np.full((NPASS * NBLK * MSLOT * 128, 128), NEGM, dtype=np.float32)
    for q in range(NPASS):
        g0 = 2048 * h + OUT_T * q - LP
        for bi, (r0, sz) in enumerate(BLOCKS):
            ws, pieces = blk_window(bi)
            i = g0 + r0 + np.arange(128)
            ivalid = (i >= 0) & (i < SEQ)
            chend = np.minimum((i // 10) * 10 + 10, SEQ)
            base = ((q * NBLK) + bi) * MSLOT * 128
            for p, (off, tp) in enumerate(pieces):
                j = g0 + ws + off + np.arange(128)
                jv = (j >= 0) & (j < SEQ) & (np.arange(128) < tp)
                allow = (ivalid[None, :] & jv[:, None]
                         & (j[:, None] >= i[None, :] - 25)
                         & (j[:, None] < chend[None, :]))
                blk = np.where(allow, 0.0, NEGM).astype(np.float32)
                out[base + p * 128: base + (p + 1) * 128, :] = blk
            colall = out[base:base + MSLOT * 128, :]
            dead = ~(colall > NEGM / 2).any(axis=0)
            out[base, dead] = 0.0
    return out.astype(ml_dtypes.bfloat16)


def kernel(L0_attn_hidden_states, L0_attn_Wq, L0_attn_bq, L0_attn_Wk, L0_attn_bk,
           L0_attn_Wv, L0_attn_bv, L0_attn_Wo, L0_attn_bo,
           L0_ffn_W_r, L0_ffn_W_k, L0_ffn_W_v, L0_ffn_mu_r, L0_ffn_mu_k):
    hid = np.asarray(L0_attn_hidden_states, dtype=np.float32)
    wmap = {"wfr": L0_ffn_W_r, "wfk": L0_ffn_W_k, "wfv": L0_ffn_W_v,
            "wq": L0_attn_Wq, "wak": L0_attn_Wk, "wav": L0_attn_Wv, "wo": L0_attn_Wo}
    wstaged = {k: np.ascontiguousarray(np.asarray(v, np.float32).T) for k, v in wmap.items()}
    vec = {"mur": L0_ffn_mu_r, "muk": L0_ffn_mu_k, "bq": L0_attn_bq, "bk": L0_attn_bk, "bo": L0_attn_bo}
    vstaged = {k: np.ascontiguousarray(np.asarray(v, np.float32).reshape(NCH, 128).T) for k, v in vec.items()}
    bv = np.ascontiguousarray(np.asarray(L0_attn_bv, np.float32).reshape(1, E))
    ident = np.eye(128, dtype=ml_dtypes.bfloat16)
    masks = {h: _stage_masks(h) for h in (0, 1)}
    cmask = {h: np.ones((128, NPASS), np.float32) for h in (0, 1)}
    cmask[0][:, 0] = 0.0

    in_maps = []
    for core in range(8):
        b, h = core // 2, core % 2
        xs = np.zeros((E, XCOLS), dtype=np.float32)
        s0 = 2048 * h - LP
        a, bnd = max(0, s0), min(SEQ, s0 + XCOLS)
        xs[:, a - s0:bnd - s0] = hid[b, a:bnd, :].T
        m = {"x": xs, "bv": bv, "ident": ident, "masks": masks[h], "cmask": cmask[h]}
        m.update(wstaged)
        m.update(vstaged)
        in_maps.append(m)

    nc = build_nc()
    res = run_bass_kernel_spmd(nc, in_maps, core_ids=list(range(8)))
    out = np.empty((NB, SEQ, E), dtype=np.float32)
    for core in range(8):
        b, h = core // 2, core % 2
        out[b, 2048 * h:2048 * (h + 1), :] = np.asarray(res.results[core]["y"]).T
    return out


# version 27
# speedup vs baseline: 1.0408x; 1.0408x over previous
import numpy as np
import ml_dtypes
import concourse.bass as bass
import concourse.mybir as mybir
from concourse.bass_utils import run_bass_kernel_spmd

F32 = mybir.dt.float32
BF16 = mybir.dt.bfloat16
AF = mybir.ActivationFunctionType
ALU = mybir.AluOpType

E = 1024
NCH = 8
SEQ = 4096
NB = 4
NPASS = 4
OUT_T = 512
LP, RP = 32, 16
P = LP + OUT_T + RP
HALF = P // 2
XCOLS = 2048 + LP + 32
BLOCKS = [(0, 128), (128, 128), (256, 128), (384, 128), (512, 48)]
NBLK = len(BLOCKS)
NEGM = -8.0e30
MSLOT = 3


def blk_window(bi):
    r0, sz = BLOCKS[bi]
    ws = max(0, r0 - 128)
    we = min(P, r0 + sz + 10)
    pieces = []
    off = 0
    while ws + off < we:
        pieces.append((off, min(128, we - (ws + off))))
        off += 128
    return ws, pieces


ENGS = ("sync", "gpsimd", "tensor", "vector", "scalar")
DMA_ENGS = ("sync", "gpsimd")


class _St:
    __slots__ = ("eng", "emit", "deps", "idx", "seq", "dma", "note")


class Prog:
    def __init__(self):
        self.steps = []
        self.lw = {}
        self.rd = {}

    def step(self, eng, emit, r=(), w=(), dma=False, note=""):
        s = _St()
        s.eng, s.emit, s.dma, s.note = eng, emit, dma, note
        s.deps = set()
        for x in r:
            d = self.lw.get(x)
            if d is not None:
                s.deps.add(d)
        for x in w:
            d = self.lw.get(x)
            if d is not None:
                s.deps.add(d)
            for dd in self.rd.get(x, ()):
                s.deps.add(dd)
        s.idx = len(self.steps)
        self.steps.append(s)
        for x in r:
            self.rd.setdefault(x, []).append(s)
        for x in w:
            self.lw[x] = s
            self.rd[x] = []
        return s

    NLANE = 8

    def emit(self, nc, block):
        cnt = {e: 0 for e in ENGS}
        dcnt = {e: 0 for e in DMA_ENGS}
        for s in self.steps:
            if s.dma:
                j = dcnt[s.eng]
                dcnt[s.eng] += 1
                s.seq = (j % self.NLANE, j // self.NLANE + 1)
            else:
                cnt[s.eng] += 1
                s.seq = cnt[s.eng]
        sems = {e: nc.alloc_semaphore("sem_" + e) for e in ENGS if cnt[e]}
        dsems = {e: [nc.alloc_semaphore("dsem_%s%d" % (e, i)) for i in range(self.NLANE)]
                 for e in DMA_ENGS if dcnt[e]}
        by_eng = {e: [s for s in self.steps if s.eng == e] for e in ENGS}

        def body(eng_obj, ename):
            wm = {}
            for s in by_eng[ename]:
                need = {}
                for d in s.deps:
                    if d.dma:
                        lane, use = d.seq
                        sem, val = dsems[d.eng][lane], 16 * use
                    else:
                        sem, val = sems[d.eng], d.seq
                    k = id(sem)
                    if need.get(k, (None, -1))[1] < val:
                        need[k] = (sem, val)
                if s.dma:
                    lane, use = s.seq
                    if use > 1:
                        sem = dsems[ename][lane]
                        k = id(sem)
                        if need.get(k, (None, -1))[1] < 16 * (use - 1):
                            need[k] = (sem, 16 * (use - 1))
                for k in sorted(need, key=lambda q: -need[q][1]):
                    sem, val = need[k]
                    if wm.get(k, 0) < val:
                        eng_obj.wait_ge(sem, val)
                        wm[k] = val
                last = s.emit(eng_obj)
                if s.dma:
                    lane, use = s.seq
                    last.then_inc(dsems[ename][lane], 16)
                else:
                    last.then_inc(sems[ename], 1)
            if ename == "sync" and dcnt["sync"]:
                for lane in range(self.NLANE):
                    n_uses = (dcnt["sync"] - lane + self.NLANE - 1) // self.NLANE
                    if n_uses:
                        eng_obj.wait_ge(dsems["sync"][lane], 16 * n_uses)

        for ename in ENGS:
            if not by_eng[ename]:
                continue
            getattr(block, ename)(lambda eng, _e=ename: body(eng, _e))


WNAMES = ("wfr", "wfk", "wfv", "wq", "wak", "wav", "wo")


def build_nc(stop_after=None):
    nc = bass.Bass("TRN2", target_bir_lowering=False)
    pr = Prog()

    x_d = nc.dram_tensor("x", [E, XCOLS], F32, kind="ExternalInput").ap()
    w_d = {w: nc.dram_tensor(w, [E, E], F32, kind="ExternalInput").ap() for w in WNAMES}
    mur_d = nc.dram_tensor("mur", [128, NCH], F32, kind="ExternalInput").ap()
    muk_d = nc.dram_tensor("muk", [128, NCH], F32, kind="ExternalInput").ap()
    bq_d = nc.dram_tensor("bq", [128, NCH], F32, kind="ExternalInput").ap()
    bk_d = nc.dram_tensor("bk", [128, NCH], F32, kind="ExternalInput").ap()
    bo_d = nc.dram_tensor("bo", [128, NCH], F32, kind="ExternalInput").ap()
    bv_d = nc.dram_tensor("bv", [1, E], F32, kind="ExternalInput").ap()
    id_d = nc.dram_tensor("ident", [128, 128], BF16, kind="ExternalInput").ap()
    mk_d = nc.dram_tensor("masks", [NPASS * NBLK * MSLOT * 128, 128], BF16, kind="ExternalInput").ap()
    cm_d = nc.dram_tensor("cmask", [128, NPASS], F32, kind="ExternalInput").ap()
    y_d = nc.dram_tensor("y", [E, 2048], F32, kind="ExternalOutput").ap()
    sc_d = {w: nc.dram_tensor("sc_" + w, [128, 2 * NCH * E], BF16, kind="Internal").ap()
            for w in WNAMES}

    A = nc.alloc_sbuf_tensor
    X = A("X", [128, NCH * P], F32)
    T = A("T", [128, NCH * P], F32)
    bfAh = A("bfAh", [128, NCH * P], BF16)
    bfAl = A("bfAl", [128, NCH * P], BF16)
    bfBh = A("bfBh", [128, NCH * P], BF16)
    bfBl = A("bfBl", [128, NCH * P], BF16)
    vth = A("vth", [128, NBLK * E], BF16)
    vtl = A("vtl", [128, NBLK * E], BF16)
    qbh = A("qbh", [128, NCH * 256], BF16)
    qbl = A("qbl", [128, NCH * 256], BF16)
    aeh = A("aeh", [128, NCH * 128], BF16)
    ael = A("ael", [128, NCH * 128], BF16)
    e32x = [A("e32%d" % i, [128, 2 * MSLOT * 128], F32) for i in range(3)]
    elx = [A("el%d" % i, [128, 2 * MSLOT * 128], BF16) for i in range(3)]
    msb = [A("msb%d" % i, [128, MSLOT * 128], BF16) for i in range(2)]
    wsl = [A("wsl%d" % i, [128, 2 * NCH * E], BF16) for i in range(2)]
    dtmp2 = [A("dtmp0", [128, HALF], F32), A("dtmp1", [128, HALF], F32)]
    xtmp2 = [A("xtmp0", [128, HALF], F32), A("xtmp1", [128, HALF], F32)]
    katmp2 = [A("katmp0", [128, HALF], F32), A("katmp1", [128, HALF], F32)]
    pvt = A("pvt", [128, 128], F32)
    mb_sb = A("mb_sb", [128, P], F32)
    rb32 = A("rb32", [128, 128], F32)
    m32 = A("m32", [1, P], F32)
    v32 = A("v32", [1, P], F32)
    rs32x = [A("rs32a", [1, 256], F32), A("rs32b", [1, 256], F32)]
    lncol = A("lncol", [128, NCH], F32)
    mu_r = A("mu_r", [128, NCH], F32)
    mu_k = A("mu_k", [128, NCH], F32)
    dmu = A("dmu", [128, NCH], F32)
    bqs = A("bqs", [128, NCH], F32)
    bks = A("bks", [128, NCH], F32)
    bos = A("bos", [128, NCH], F32)
    bvh = A("bvh", [1, E], BF16)
    bvl = A("bvl", [1, E], BF16)
    idt = A("idt", [128, 128], BF16)
    cms = A("cms", [128, NPASS], F32)
    _vthA = nc.lookup_mloc(vth).addr
    _vtlA = nc.lookup_mloc(vtl).addr
    wstg = [nc.alloc_sbuf_tensor_at("wstg%d" % i, [128, 1024], F32, offset=_vthA + 4096 * i)
            for i in range(2)]
    whib = [nc.alloc_sbuf_tensor_at("whib%d" % i, [128, 1024], BF16, offset=_vtlA + 4096 + 4096 * i)
            for i in range(2)]
    wlob = [nc.alloc_sbuf_tensor_at("wlob0", [128, 1024], BF16, offset=_vtlA + 6144),
            nc.alloc_sbuf_tensor_at("wlob1", [128, 1024], BF16, offset=_vthA + 8192)]
    WSTG_R = [["wstg0", "vth0", "vth1"], ["wstg1", "vth2", "vth3"]]
    WHIB_R = [["whib0", "vtl2"], ["whib1", "vtl4"]]
    WLOB_R = [["wlob0", "vtl3"], ["wlob1", "vth4"]]
    bvs = nc.alloc_sbuf_tensor_at("bvs", [1, E], F32, offset=_vtlA)
    onec = A("onec", [1, 128], BF16)
    onecf = A("onecf", [1, 128], F32)
    onem = A("onem", [128, 1], BF16)
    onemf = A("onemf", [128, 1], F32)
    ones = A("ones", [128, 1], BF16)
    epsap = A("epsap", [1, 1], F32)
    nhalf = A("nhalf", [1, 1], F32)
    sc8 = A("sc8", [128, 1], F32)
    nb12 = A("nb12", [128, 1], F32)
    none1 = A("none1", [1, 1], F32)

    PA = nc.alloc_psum_tensor
    gp = [PA("gp0", [128, 512], F32), PA("gp1", [128, 512], F32)]
    sb = [PA("sb0", [128, 512], F32), PA("sb1", [128, 512], F32)]
    pv = [PA("pv0", [128, 512], F32), PA("pv1", [128, 512], F32)]
    misc = PA("misc", [128, 512], F32)
    bca = PA("bca", [128, 512], F32)

    def thi_chs(ci, a, b):
        return T.ap().bitcast(BF16).rearrange("p (c t two) -> p c t two", c=NCH, two=2)[:, ci, a:b, 1]

    def thi_cols(a, b):
        return T.ap().bitcast(BF16).rearrange("p (n two) -> p n two", two=2)[:, a:b, 1]

    def thi_sview(a, b):
        return T.ap().bitcast(BF16).rearrange("p (c t two) -> p c t two", c=NCH, two=2)[:, :, a:b, 1]

    def ehv(pb):
        return e32x[pb].ap().bitcast(BF16).rearrange("p (n two) -> p n two", two=2)[:, :, 1]

    _cap = [None]

    def st(eng, emit, r=(), w=(), dma=False, note=""):
        if _cap[0] is not None:
            _cap[0].append((eng, emit, r, w, dma, note))
            return None
        return pr.step(eng, emit, r=r, w=w, dma=dma, note=note)

    def capture(fn, *a, **k):
        q = []
        prev = _cap[0]
        _cap[0] = q
        try:
            fn(*a, **k)
        finally:
            _cap[0] = prev
        return q

    def replay(steps):
        for (eng, emit, r, w, dma, note) in steps:
            pr.step(eng, emit, r=r, w=w, dma=dma, note=note)

    wk_ctr = [0]

    def warmkeep(dep_res):
        wk_ctr[0] += 1
        st("tensor", lambda t: t.matmul(bca[0:1, 508:512], onec[0:1, 0:1], onec[0:1, 0:4],
                                        start=True, stop=True),
           r=list(dep_res), w=["bca"], note="warmkeep%d" % wk_ctr[0])

    def weave(chain, bg_groups, lead=2):
        nt = sum(1 for s in chain if s[0] == "tensor")
        per = max(1, (len(bg_groups) - lead) // max(1, nt + 1))
        gi = 0
        for j in range(min(lead, len(bg_groups))):
            replay(bg_groups[gi])
            gi += 1
        k = 0
        for s in chain:
            if s[0] == "tensor":
                for j in range(per):
                    if gi < len(bg_groups):
                        replay(bg_groups[gi])
                        gi += 1
            pr.step(s[0], s[1], r=s[2], w=s[3], dma=s[4], note=s[5])
        for j in range(gi, len(bg_groups)):
            replay(bg_groups[j])

    def weave_even(chain, bg_groups):
        nt = sum(1 for s in chain if s[0] == "tensor")
        if not bg_groups:
            replay(chain)
            return
        stride = max(1, nt // (len(bg_groups) + 1))
        gi = 0
        k = 0
        for s in chain:
            if s[0] == "tensor":
                k += 1
                if k % stride == 0 and gi < len(bg_groups):
                    replay(bg_groups[gi])
                    gi += 1
            pr.step(s[0], s[1], r=s[2], w=s[3], dma=s[4], note=s[5])
        for j in range(gi, len(bg_groups)):
            replay(bg_groups[j])

    def groups_of(fn, *a, **k):
        q = capture(fn, *a, **k)
        gs = []
        for s in q:
            if s[0] == "tensor" or not gs:
                gs.append([s])
            else:
                gs[-1].append(s)
        return gs

    def init_v(v):
        v.memset(onec[:, :], 1.0)
        v.memset(onecf[:, :], 1.0)
        v.memset(onem[:, :], 1.0 / 1024.0)
        v.memset(onemf[:, :], 1.0 / 1024.0)
        v.memset(ones[:, :], 1.0)
        v.memset(epsap[:, :], 1e-5)
        v.memset(nhalf[:, :], -0.5)
        v.memset(sc8[:, :], 0.125)
        v.memset(nb12[:, :], -12.0)
        v.memset(none1[:, :], -1.0)
        for t in gp + sb + pv + [misc, bca]:
            v.memset(t[:, :], 1.0)
        v.memset(rs32x[0][:, :], 1.0)
        return v.memset(rs32x[1][:, :], 1.0)
    st("vector", init_v, w=["consts", "gp0", "gp1", "s0", "s1", "pv0", "pv1", "misc", "bca"], note="init")

    st("sync", lambda s: s.dma_start(out=idt[:, :], in_=id_d), w=["idt"], dma=True, note="ident")
    st("sync", lambda s: s.dma_start(out=mu_r[:, :], in_=mur_d), w=["mu"], dma=True)
    st("sync", lambda s: s.dma_start(out=mu_k[:, :], in_=muk_d), w=["mu"], dma=True)
    st("sync", lambda s: s.dma_start(out=bqs[:, :], in_=bq_d), w=["battn"], dma=True)
    st("sync", lambda s: s.dma_start(out=bks[:, :], in_=bk_d), w=["battn"], dma=True)
    st("sync", lambda s: s.dma_start(out=bos[:, :], in_=bo_d), w=["battn"], dma=True)
    st("sync", lambda s: s.dma_start(out=bvs[:, :], in_=bv_d), w=["bv32"], dma=True)
    st("sync", lambda s: s.dma_start(out=cms[:, :], in_=cm_d), w=["cms"], dma=True)
    st("scalar", lambda s: s.copy(out=bvh[:, :], in_=bvs[:, :]), r=["bv32"], w=["bvh"])
    st("vector", lambda v: v.tensor_tensor(out=dmu[:, :], in0=mu_r[:, :], in1=mu_k[:, :], op=ALU.subtract),
       r=["mu"], w=["mu"])
    st("vector", lambda v: v.tensor_tensor(out=bvl[:, :], in0=bvs[:, :], in1=bvh[:, :], op=ALU.subtract),
       r=["bv32", "bvh"], w=["bvl"])

    def wsplit_chunk(w, ci):
        wsb = wstg[ci % 2]
        hib, lob = whib[ci % 2], wlob[ci % 2]
        sres, hres_, lres = WSTG_R[ci % 2], WHIB_R[ci % 2], WLOB_R[ci % 2]
        st("sync", lambda s, _w=w, _ci=ci, _d=wsb: s.dma_start(
            out=_d[:, :], in_=w_d[_w][_ci * 128:(_ci + 1) * 128, :]),
           w=sres, dma=True, note="Wld %s c%d" % (w, ci))
        st("scalar", lambda s, _o=hib, _i=wsb: s.copy(out=_o[:, :], in_=_i[:, :]),
           r=sres, w=hres_, note="Wsp hi %s c%d" % (w, ci))
        st("vector", lambda v, _o=lob, _i=wsb, _h=hib:
           v.tensor_tensor(out=_o[:, :], in0=_i[:, :], in1=_h[:, :], op=ALU.subtract),
           r=sres + hres_, w=lres, note="Wsp lo %s c%d" % (w, ci))
        st("gpsimd", lambda s, _w=w, _h=hib, _ci=ci: s.dma_start(
            out=sc_d[_w][:, _ci * E:(_ci + 1) * E], in_=_h[:, :]),
           r=hres_, w=["sc_%s_%d" % (w, ci)], dma=True, note="Wst %s h c%d" % (w, ci))
        st("gpsimd", lambda s, _w=w, _l=lob, _ci=ci: s.dma_start(
            out=sc_d[_w][:, NCH * E + _ci * E:NCH * E + (_ci + 1) * E], in_=_l[:, :]),
           r=lres, w=["sc_%s_%d" % (w, 8 + ci)], dma=True, note="Wst %s l c%d" % (w, ci))

    phaseW_groups = []
    for w in WNAMES:
        for ci in range(NCH):
            phaseW_groups.append(capture(wsplit_chunk, w, ci))

    use_ctr = [0]

    def load_use(w):
        k = use_ctr[0]
        use_ctr[0] += 1
        slot = wsl[k % 2]
        sname = "wsl%d" % (k % 2)
        st("gpsimd", lambda s, _w=w, _slot=slot: s.dma_start(out=_slot[:, :], in_=sc_d[_w]),
           r=["sc_%s_%d" % (w, j) for j in range(16)],
           w=[sname + "_hi", sname + "_lo"], dma=True, note="use-ld " + w)
        return slot, [sname + "_hi", sname + "_lo"]

    def wap(slot, reg, ci, d0, dn):
        base = reg * NCH * E + ci * E + d0
        return slot[:, base:base + dn]

    def chs(buf, ci, a, b):
        return buf[:, ci * P + a:ci * P + b]

    def hres(base, a, b):
        out = []
        if a < HALF:
            out.append(base + "0")
        if b > HALF:
            out.append(base + "1")
        return out

    mmbank = [("gp0", gp[0]), ("gp1", gp[1]), ("s0", sb[0]), ("s1", sb[1])]
    mmctr = [0]

    def nxbank():
        i = mmctr[0] % 4
        mmctr[0] += 1
        return mmbank[i]

    def sview(buf, a, b):
        return buf.ap().rearrange("p (c t) -> p c t", c=NCH)[:, :, a:b]

    def emit_ln_half(src, sres_base, hh, passlbl, want_hl=False):
        lbl = "%s.h%d" % (passlbl, hh)
        a, b = hh * HALF, (hh + 1) * HALF
        rT = "T%d" % hh
        rAh, rAl = "bfAh%d" % hh, "bfAl%d" % hh
        rBh, rBl = "bfBh%d" % hh, "bfBl%d" % hh
        rsrc = sres_base + "%d" % hh

        def mmm(t):
            ins = None
            for ci in range(NCH):
                ins = t.matmul(misc[0:1, 0:HALF], onemf[:, :], chs(src, ci, a, b),
                               start=(ci == 0), stop=(ci == NCH - 1))
            return ins
        st("tensor", mmm, r=[rsrc, "consts"], w=["misc"], note=lbl + " mean mm")
        st("vector", lambda v: v.tensor_copy(out=m32[:, a:b], in_=misc[0:1, 0:HALF]),
           r=["misc"], w=["m32" + str(hh)], note=lbl + " mean ev")
        st("tensor", lambda t: t.matmul(bca[:, 0:HALF], onecf[0:1, :], m32[:, a:b], start=True, stop=True),
           r=["m32" + str(hh), "consts"], w=["bca"], note=lbl + " mean bc")
        st("vector", lambda v: v.tensor_copy(out=mb_sb[:, a:b], in_=bca[:, 0:HALF]),
           r=["bca"], w=["mb" + str(hh)], note=lbl + " mb ev")

        st("vector", lambda v: v.tensor_tensor(
            out=sview(T, a, b), in0=sview(src, a, b),
            in1=mb_sb.ap()[:, a:b].unsqueeze(1).broadcast_to([128, NCH, b - a]), op=ALU.subtract),
           r=[rsrc, "mb" + str(hh)], w=[rT], note=lbl + " tsub")

        for ci in range(NCH):
            sqb = dtmp2[hh] if ci % 2 == 0 else xtmp2[hh]
            rsb = ("dtmp%d" % hh) if ci % 2 == 0 else ("xtmp%d" % hh)
            st("scalar", lambda s, _ci=ci, _b=sqb: s.activation(out=_b[:, :], in_=chs(T, _ci, a, b),
                                                                func=AF.Square),
               r=[rT], w=[rsb], note=lbl + " sq%d" % ci)
            st("scalar", lambda s, _ci=ci, _b=sqb: s.copy(out=chs(bfBh, _ci, a, b), in_=_b[:, :]),
               r=[rsb], w=[rBh], note=lbl + " sqh%d" % ci)
            st("vector", lambda v, _ci=ci, _b=sqb: v.tensor_tensor(out=chs(bfBl, _ci, a, b), in0=_b[:, :],
                                                                   in1=chs(bfBh, _ci, a, b), op=ALU.subtract),
               r=[rsb, rBh], w=[rBl], note=lbl + " sql%d" % ci)

        def mmv(t):
            first = True
            ins = None
            for ci in range(NCH):
                for bb in (bfBh, bfBl):
                    ins = t.matmul(misc[32:33, 0:HALF], onem[:, :], chs(bb, ci, a, b),
                                   start=first, stop=(ci == NCH - 1 and bb is bfBl))
                    first = False
            return ins
        st("tensor", mmv, r=[rBh, rBl, "consts"], w=["misc"], note=lbl + " var mm")
        st("vector", lambda v: v.tensor_copy(out=v32[:, a:b], in_=misc[32:33, 0:HALF]),
           r=["misc"], w=["v32" + str(hh)], note=lbl + " var ev")
        st("scalar", lambda s: s.activation(out=v32[:, a:b], in_=v32[:, a:b], func=AF.Ln,
                                            bias=epsap[0:1, :], scale=1.0),
           r=["v32" + str(hh), "consts"], w=["v32" + str(hh)], note=lbl + " lnv")
        st("scalar", lambda s: s.activation(out=v32[:, a:b], in_=v32[:, a:b], func=AF.Exp,
                                            bias=0.0, scale=nhalf[0:1, :]),
           r=["v32" + str(hh), "consts"], w=["v32" + str(hh)], note=lbl + " expv")
        st("tensor", lambda t: t.matmul(bca[:, 0:HALF], onecf[0:1, :], v32[:, a:b], start=True, stop=True),
           r=["v32" + str(hh), "consts"], w=["bca"], note=lbl + " r bc")
        st("vector", lambda v: v.tensor_copy(out=mb_sb[:, a:b], in_=bca[:, 0:HALF]),
           r=["bca"], w=["mb" + str(hh)], note=lbl + " rb ev")

        st("vector", lambda v: v.tensor_tensor(
            out=sview(T, a, b), in0=sview(T, a, b),
            in1=mb_sb.ap()[:, a:b].unsqueeze(1).broadcast_to([128, NCH, b - a]), op=ALU.mult),
           r=["mb" + str(hh)], w=[rT], note=lbl + " ln mul")
        if hh == 0 and not want_hl:
            st("vector", lambda v: v.tensor_copy(out=lncol.ap().unsqueeze(2),
                                                 in_=sview(T, HALF - 1, HALF)),
               r=[rT], w=["lncol"], note=lbl + " lncol")
        if want_hl:
            st("vector", lambda v: v.tensor_tensor(out=sview(bfAl, a, b), in0=sview(T, a, b),
                                                   in1=thi_sview(a, b), op=ALU.subtract),
               r=[rT], w=[rAl], note=lbl + " ln lo")

    def emit_mixes_half(hh, passlbl):
        lbl = "%s.h%d" % (passlbl, hh)
        a, b = hh * HALF, (hh + 1) * HALF
        a1 = a + 1
        rdeps = ["T%d" % hh] + (["lncol"] if hh == 1 else [])
        rAh, rAl = "bfAh%d" % hh, "bfAl%d" % hh
        rBh, rBl = "bfBh%d" % hh, "bfBl%d" % hh
        for ci in range(NCH):
            dtm = dtmp2[hh] if ci % 2 == 0 else xtmp2[hh]
            rd = ("dtmp%d" % hh) if ci % 2 == 0 else ("xtmp%d" % hh)
            xtm = katmp2[hh] if ci % 2 == 0 else dtmp2[1 - hh]
            rx = ("katmp%d" % hh) if ci % 2 == 0 else ("dtmp%d" % (1 - hh))
            def dd(v, _ci=ci, _dtm=dtm):
                ins = v.tensor_tensor(out=_dtm[:, 1:b - a], in0=chs(T, _ci, a1, b),
                                      in1=chs(T, _ci, a1 - 1, b - 1), op=ALU.subtract)
                if hh == 0:
                    ins = v.memset(_dtm[:, 0:1], 0.0)
                else:
                    ins = v.tensor_tensor(out=_dtm[:, 0:1], in0=chs(T, _ci, a, a + 1),
                                          in1=lncol[:, _ci:_ci + 1], op=ALU.subtract)
                return ins
            st("vector", dd, r=rdeps, w=[rd], note=lbl + " d%d" % ci)
            def xk(v, _ci=ci, _dtm=dtm, _xtm=xtm):
                ins = v.scalar_tensor_tensor(out=_xtm[:, 1:b - a], in0=_dtm[:, 1:b - a],
                                             scalar=mu_k[:, _ci:_ci + 1], in1=chs(T, _ci, a1 - 1, b - 1),
                                             op0=ALU.mult, op1=ALU.add)
                if hh == 0:
                    ins = v.memset(_xtm[:, 0:1], 0.0)
                else:
                    ins = v.scalar_tensor_tensor(out=_xtm[:, 0:1], in0=_dtm[:, 0:1],
                                                 scalar=mu_k[:, _ci:_ci + 1], in1=lncol[:, _ci:_ci + 1],
                                                 op0=ALU.mult, op1=ALU.add)
                return ins
            st("vector", xk, r=[rd, "mu"] + rdeps, w=[rx], note=lbl + " xk%d" % ci)
            st("scalar", lambda s, _ci=ci, _xtm=xtm: s.copy(out=chs(bfBh, _ci, a, b), in_=_xtm[:, :]),
               r=[rx], w=[rBh], note=lbl + " xkh%d" % ci)
            st("gpsimd", lambda g, _ci=ci, _xtm=xtm: g.tensor_tensor(out=chs(bfBl, _ci, a, b), in0=_xtm[:, :],
                                                                      in1=chs(bfBh, _ci, a, b), op=ALU.subtract),
               r=[rx, rBh], w=[rBl], note=lbl + " xkl%d" % ci)
            def xr(v, _ci=ci, _dtm=dtm, _xtm=xtm):
                return v.scalar_tensor_tensor(out=_dtm[:, 0:b - a], in0=_dtm[:, 0:b - a],
                                              scalar=dmu[:, _ci:_ci + 1], in1=_xtm[:, 0:b - a],
                                              op0=ALU.mult, op1=ALU.add)
            st("vector", xr, r=[rd, rx, "mu"], w=[rd], note=lbl + " xr%d" % ci)
            st("scalar", lambda s, _ci=ci, _dtm=dtm: s.copy(out=chs(bfAh, _ci, a, b), in_=_dtm[:, :]),
               r=[rd], w=[rAh], note=lbl + " xrh%d" % ci)
            st("gpsimd", lambda g, _ci=ci, _dtm=dtm: g.tensor_tensor(out=chs(bfAl, _ci, a, b), in0=_dtm[:, :],
                                                                      in1=chs(bfAh, _ci, a, b), op=ALU.subtract),
               r=[rd, rAh], w=[rAl], note=lbl + " xrl%d" % ci)

    def emit_mm24(t, out_ap, slot, d0, dn, rhsh, rhsl):
        first = True
        ins = None
        for ci in range(NCH):
            combos = ((0, rhsh(ci)), (1, rhsh(ci)), (0, rhsl(ci)))
            for j, (reg, rhs) in enumerate(combos):
                ins = t.matmul(out_ap, wap(slot, reg, ci, d0, dn), rhs,
                               start=first, stop=(ci == NCH - 1 and j == 2))
                first = False
        return ins

    def ffn_proj_half(w_use, hh, kind, passlbl):
        slot, sres = w_use
        lbl = "%s.%s.h%d" % (passlbl, kind, hh)
        a, b = hh * HALF, (hh + 1) * HALF
        src_h, src_l = (bfBh, bfBl) if kind == "k" else (bfAh, bfAl)
        rsh_, rsl_ = (("bfBh%d" % hh, "bfBl%d" % hh) if kind == "k"
                      else ("bfAh%d" % hh, "bfAl%d" % hh))
        ktm = katmp2[hh]
        rkt = "katmp%d" % hh
        for dj in range(NCH):
            bres, bank = nxbank()
            def mm(t, _dj=dj, _bank=bank, _slot=slot):
                return emit_mm24(t, _bank[:, 0:HALF], _slot, _dj * 128, 128,
                                 lambda ci: chs(src_h, ci, a, b),
                                 lambda ci: chs(src_l, ci, a, b))
            st("tensor", mm, r=[rsh_, rsl_] + sres, w=[bres], note=lbl + " mm%d" % dj)
            if kind == "r":
                st("scalar", lambda s, _dj=dj, _bank=bank:
                   s.activation(out=chs(T, _dj, a, b), in_=_bank[:, 0:HALF], func=AF.Sigmoid),
                   r=[bres], w=["T%d" % hh], note=lbl + " sig%d" % dj)
            elif kind == "k":
                st("scalar", lambda s, _bank=bank: s.activation(out=ktm[:, :], in_=_bank[:, 0:HALF],
                                                                func=AF.Relu),
                   r=[bres], w=[rkt], note=lbl + " krel%d" % dj)
                st("scalar", lambda s: s.activation(out=ktm[:, :], in_=ktm[:, :], func=AF.Square),
                   r=[rkt], w=[rkt], note=lbl + " ksq%d" % dj)
                st("scalar", lambda s, _dj=dj: s.copy(out=chs(bfAh, _dj, a, b), in_=ktm[:, :]),
                   r=[rkt], w=["bfAh%d" % hh], note=lbl + " kh%d" % dj)
                st("vector", lambda v, _dj=dj: v.tensor_tensor(out=chs(bfAl, _dj, a, b), in0=ktm[:, :],
                                                               in1=chs(bfAh, _dj, a, b), op=ALU.subtract),
                   r=[rkt, "bfAh%d" % hh], w=["bfAl%d" % hh], note=lbl + " kl%d" % dj)
            else:
                st("vector", lambda v, _dj=dj, _bank=bank:
                   v.scalar_tensor_tensor(out=ktm[:, :], in0=_bank[:, 0:HALF], scalar=0.5,
                                          in1=chs(T, _dj, a, b), op0=ALU.mult, op1=ALU.mult),
                   r=[bres, "T%d" % hh], w=[rkt], note=lbl + " vsc%d" % dj)
                st("vector", lambda v, _dj=dj:
                   v.tensor_tensor(out=chs(X, _dj, a, b), in0=chs(X, _dj, a, b),
                                   in1=ktm[:, :], op=ALU.add),
                   r=[rkt], w=["X%d" % hh], note=lbl + " xadd%d" % dj)

    def attn_K_half(w_use, hh, passlbl):
        slot, sres = w_use
        lbl = "%s.K.h%d" % (passlbl, hh)
        a, b = hh * HALF, (hh + 1) * HALF
        for dj in range(NCH):
            bres, bank = nxbank()
            def mm(t, _dj=dj, _bank=bank, _slot=slot):
                return emit_mm24(t, _bank[:, 0:HALF], _slot, _dj * 128, 128,
                                 lambda ci: thi_chs(ci, a, b),
                                 lambda ci: chs(bfAl, ci, a, b))
            st("tensor", mm, r=["T%d" % hh, "bfAl%d" % hh] + sres, w=[bres], note=lbl + " mm%d" % dj)
            st("scalar", lambda s, _dj=dj, _bank=bank:
               s.add(out=chs(bfBh, _dj, a, b), in_=_bank[:, 0:HALF], add=bks[:, _dj:_dj + 1]),
               r=[bres, "battn"], w=["bfBh%d" % hh], note=lbl + " kh%d" % dj)
            st("vector", lambda v, _dj=dj, _bank=bank:
               v.scalar_tensor_tensor(out=chs(bfBl, _dj, a, b), in0=_bank[:, 0:HALF],
                                      scalar=bks[:, _dj:_dj + 1], in1=chs(bfBh, _dj, a, b),
                                      op0=ALU.add, op1=ALU.subtract),
               r=[bres, "battn", "bfBh%d" % hh], w=["bfBl%d" % hh], note=lbl + " kl%d" % dj)

    def attn_V_tile(w_use, ti, passlbl):
        slot, sres = w_use
        t0, tsz = BLOCKS[ti]
        lbl = "%s.V.t%d" % (passlbl, ti)
        lnres = hres("T", t0, t0 + tsz) + hres("bfAl", t0, t0 + tsz)
        for dh in range(2):
            bres, bank = nxbank()
            def mm(t, _t0=t0, _tsz=tsz, _dh=dh, _bank=bank, _slot=slot):
                first = True
                for ci in range(NCH):
                    for reg, hi in ((0, True), (1, True), (0, False)):
                        lhs = (thi_cols(ci * P + _t0, ci * P + _t0 + _tsz) if hi
                               else bfAl[:, ci * P + _t0:ci * P + _t0 + _tsz])
                        t.matmul(_bank[0:_tsz, 0:512], lhs,
                                 wap(_slot, reg, ci, _dh * 512, 512), start=first, stop=False)
                        first = False
                t.matmul(_bank[0:_tsz, 0:512], onec[0:1, 0:_tsz], bvh[:, _dh * 512:(_dh + 1) * 512],
                         start=False, stop=False)
                return t.matmul(_bank[0:_tsz, 0:512], onec[0:1, 0:_tsz], bvl[:, _dh * 512:(_dh + 1) * 512],
                                start=False, stop=True)
            st("tensor", mm, r=lnres + ["bvh", "bvl", "consts"] + sres, w=[bres], note=lbl + " mm%d" % dh)
            st("scalar", lambda s, _ti=ti, _tsz=tsz, _dh=dh, _bank=bank:
               s.copy(out=vth[0:_tsz, _ti * E + _dh * 512:_ti * E + (_dh + 1) * 512],
                      in_=_bank[0:_tsz, 0:512]),
               r=[bres], w=["vth%d" % ti], note=lbl + " vh%d" % dh)
            st("vector", lambda v, _ti=ti, _tsz=tsz, _dh=dh, _bank=bank:
               v.tensor_tensor(out=vtl[0:_tsz, _ti * E + _dh * 512:_ti * E + (_dh + 1) * 512],
                               in0=_bank[0:_tsz, 0:512],
                               in1=vth[0:_tsz, _ti * E + _dh * 512:_ti * E + (_dh + 1) * 512],
                               op=ALU.subtract),
               r=[bres, "vth%d" % ti], w=["vtl%d" % ti], note=lbl + " vl%d" % dh)

    def attn_Q_pair(qslot, qsres, bi0, nb2, passlbl):
        lbl = "%s.Qp%d" % (passlbl, bi0)
        r0 = BLOCKS[bi0][0]
        n = sum(BLOCKS[bi0 + j][1] for j in range(nb2))
        lnres = hres("T", r0, r0 + n) + hres("bfAl", r0, r0 + n)
        for dj in range(NCH):
            g = dj % 2
            def mm(t, _dj=dj, _g=g, _r0=r0, _n=n, _slot=qslot):
                return emit_mm24(t, gp[_g][:, 0:_n], _slot, _dj * 128, 128,
                                 lambda ci: thi_cols(ci * P + _r0, ci * P + _r0 + _n),
                                 lambda ci: bfAl[:, ci * P + _r0:ci * P + _r0 + _n])
            st("tensor", mm, r=lnres + qsres, w=["gp%d" % g], note=lbl + " Q%d" % dj)
            st("scalar", lambda s, _dj=dj, _g=g, _n=n:
               s.add(out=qbh[:, _dj * 256:_dj * 256 + _n], in_=gp[_g][:, 0:_n], add=bqs[:, _dj:_dj + 1]),
               r=["gp%d" % g, "battn"], w=["qbh"], note=lbl + " qh%d" % dj)
            st("vector", lambda v, _dj=dj, _g=g, _n=n:
               v.scalar_tensor_tensor(out=qbl[:, _dj * 256:_dj * 256 + _n], in0=gp[_g][:, 0:_n],
                                      scalar=bqs[:, _dj:_dj + 1], in1=qbh[:, _dj * 256:_dj * 256 + _n],
                                      op0=ALU.add, op1=ALU.subtract),
               r=["gp%d" % g, "battn", "qbh"], w=["qbl"], note=lbl + " ql%d" % dj)

    def attn_pairs(q, bi, qoff, qslot, qsres, passlbl, preempt=None):
        lbl = "%s.b%d" % (passlbl, bi)
        r0, sz = BLOCKS[bi]
        ws, pieces = blk_window(bi)
        npc = len(pieces) * 128
        mbuf = msb[bi % 2]
        mres = "msb%d" % (bi % 2)
        lnres = hres("bfAh", r0, r0 + sz) + hres("bfAl", r0, r0 + sz)
        kres = hres("bfBh", ws, ws + npc) + hres("bfBl", ws, ws + npc)

        def mld(s, _q=q, _bi=bi, _mb=mbuf):
            off = ((_q * NBLK) + _bi) * MSLOT * 128
            return s.dma_start(out=_mb.ap().rearrange("p (c s) -> p c s", c=MSLOT),
                               in_=mk_d[off:off + MSLOT * 128, :].rearrange("(c p) s -> p c s", p=128))
        st("sync", mld, w=[mres], dma=True, note=lbl + " mask")

        def head_stage(pr):
            pb = pr % 3
            e32b, elb = e32x[pb], elx[pb]
            for par in range(2):
                def mm(t, _pr=pr, _par=par):
                    p0 = 64 * _par
                    for p, (off, tp) in enumerate(pieces):
                        t.matmul(sb[_par][0:128, p * 128:(p + 1) * 128], idt[:, :],
                                 mbuf[:, p * 128:(p + 1) * 128], start=(p == 0), stop=False)
                        for (kb, qb) in ((bfBh, qbh), (bfBh, qbl), (bfBl, qbh)):
                            t.matmul(sb[_par][0:tp, p * 128:p * 128 + sz],
                                     kb[p0:p0 + 64, _pr * P + ws + off:_pr * P + ws + off + tp],
                                     qb[p0:p0 + 64, _pr * 256 + qoff:_pr * 256 + qoff + sz],
                                     start=False, stop=False, tile_position=(p0, 0))
                    return t.matmul(sb[_par][0:128, 0:128], idt[:, :], mbuf[:, 0:128],
                                    start=False, stop=True)
                st("tensor", mm, r=kres + ["qbh", "qbl", mres, "idt"], w=["s%d" % par],
                   note=lbl + " sc p%d.%d" % (pr, par))
            def ex(s, _pb=pb):
                s.activation(out=e32b[:, 0:npc], in_=sb[0][:, 0:npc],
                             func=AF.Exp, bias=nb12[:, :], scale=sc8[:, :])
                return s.activation(out=e32b[:, 384:384 + npc], in_=sb[1][:, 0:npc],
                                    func=AF.Exp, bias=nb12[:, :], scale=sc8[:, :])
            st("scalar", ex, r=["s0", "s1", "consts"], w=["e32_%d" % pb],
               note=lbl + " exp p%d" % pr)
            def el_(v, _pb=pb):
                v.tensor_tensor(out=elb[:, 0:npc], in0=e32b[:, 0:npc],
                                in1=ehv(_pb)[:, 0:npc], op=ALU.subtract)
                return v.tensor_tensor(out=elb[:, 384:384 + npc], in0=e32b[:, 384:384 + npc],
                                       in1=ehv(_pb)[:, 384:384 + npc], op=ALU.subtract)
            st("vector", el_, r=["e32_%d" % pb], w=["el_%d" % pb], note=lbl + " el p%d" % pr)

        for pr in range(NCH):
            head_stage(pr)
            if pr > 0:
                tail_a(pr - 1, lbl, sz, ws, pieces)
            if pr == 2 and preempt is not None:
                preempt()
            if pr > 1:
                tail_b(pr - 2, lbl, sz, ws, pieces)
        tail_a(NCH - 1, lbl, sz, ws, pieces)
        tail_b(NCH - 2, lbl, sz, ws, pieces)
        tail_b(NCH - 1, lbl, sz, ws, pieces)

    def attn_wo(q, bi, oslot, osres, passlbl):
        lbl = "%s.b%d" % (passlbl, bi)
        r0, sz = BLOCKS[bi]
        for dj in range(NCH):
            g = dj % 2
            def mm(t, _dj=dj, _g=g, _sz=sz, _oslot=oslot):
                return emit_mm24(t, gp[_g][:, 0:_sz], _oslot, _dj * 128, 128,
                                 lambda ci: aeh[:, ci * 128:ci * 128 + _sz],
                                 lambda ci: ael[:, ci * 128:ci * 128 + _sz])
            st("tensor", mm, r=["aeh", "ael"] + osres, w=["gp%d" % g], note=lbl + " Wo%d" % dj)
            st("vector", lambda v, _dj=dj, _g=g, _r0=r0, _sz=sz:
               v.scalar_tensor_tensor(out=X[:, _dj * P + _r0:_dj * P + _r0 + _sz],
                                      in0=gp[_g][:, 0:_sz], scalar=bos[:, _dj:_dj + 1],
                                      in1=X[:, _dj * P + _r0:_dj * P + _r0 + _sz],
                                      op0=ALU.add, op1=ALU.add),
               r=["gp%d" % g, "battn"], w=hres("X", r0, r0 + sz), note=lbl + " Woev%d" % dj)

    def tail_a(pr, lbl, sz, ws, pieces):
        pb = pr % 3
        for par in range(2):
            def mm(t, _par=par, _pb=pb, _sz=sz, _pieces=pieces):
                first = True
                ins = None
                r0p = 32 * _par
                for p, (off, tp) in enumerate(_pieces):
                    for hi in (True, False):
                        ee = ehv(_pb) if hi else elx[_pb]
                        ins = t.matmul(misc[r0p:r0p + 1, 384:384 + _sz], ones[0:tp, 0:1],
                                       ee[0:tp, _par * 384 + p * 128:_par * 384 + p * 128 + _sz],
                                       start=first, stop=(p == len(_pieces) - 1 and not hi))
                        first = False
                return ins
            st("tensor", mm, r=["e32_%d" % pb, "el_%d" % pb, "consts"], w=["misc"],
               note=lbl + " sums p%d.%d" % (pr, par))
        rsb = rs32x[pr % 2]
        def rsln(s, _sz=sz, _rsb=rsb):
            s.activation(out=_rsb[:, 0:_sz], in_=misc[0:1, 384:384 + _sz], func=AF.Ln)
            return s.activation(out=_rsb[:, 128:128 + _sz], in_=misc[32:33, 384:384 + _sz], func=AF.Ln)
        st("scalar", rsln, r=["misc", "consts"], w=["rs32_%d" % (pr % 2)], note=lbl + " rsln p%d" % pr)
        def rsex(s, _sz=sz, _rsb=rsb):
            s.activation(out=_rsb[:, 0:_sz], in_=_rsb[:, 0:_sz], func=AF.Exp,
                         bias=0.0, scale=none1[0:1, :])
            return s.activation(out=_rsb[:, 128:128 + _sz], in_=_rsb[:, 128:128 + _sz], func=AF.Exp,
                                bias=0.0, scale=none1[0:1, :])
        st("scalar", rsex, r=["rs32_%d" % (pr % 2), "consts"], w=["rs32_%d" % (pr % 2)],
           note=lbl + " rsex p%d" % pr)

    def tail_b(pr, lbl, sz, ws, pieces):
        pb = pr % 3
        rsb = rs32x[pr % 2]
        def bcr(t, _sz=sz, _rsb=rsb):
            t.matmul(bca[0:64, 0:_sz], onecf[0:1, 0:64], _rsb[:, 0:_sz], start=True, stop=True)
            return t.matmul(bca[64:128, 0:_sz], onecf[0:1, 0:64], _rsb[:, 128:128 + _sz],
                            start=True, stop=True, tile_position=(0, 64))
        st("tensor", bcr, r=["rs32_%d" % (pr % 2), "consts"], w=["bca"], note=lbl + " rbc p%d" % pr)
        st("vector", lambda v, _sz=sz: v.tensor_copy(out=rb32[:, 0:_sz], in_=bca[:, 0:_sz]),
           r=["bca"], w=["rb32"], note=lbl + " rbe p%d" % pr)
        bk_i = pr % 2
        def mmpv(t, _pr=pr, _bk=bk_i, _pb=pb, _sz=sz, _ws=ws, _pieces=pieces):
            ins = None
            for par in range(2):
                hq = 2 * _pr + par
                firstmm = True
                for p, (off, tp) in enumerate(_pieces):
                    ti = (_ws + off) // 128
                    for j, (vb, hi) in enumerate(((vth, True), (vth, False), (vtl, True))):
                        ee = ehv(_pb) if hi else elx[_pb]
                        ins = t.matmul(
                            pv[_bk][64 * par:64 * par + 64, 0:_sz],
                            vb[0:tp, ti * E + 64 * hq:ti * E + 64 * hq + 64],
                            ee[0:tp, par * 384 + p * 128:par * 384 + p * 128 + _sz],
                            start=firstmm, stop=(p == len(_pieces) - 1 and j == 2),
                            tile_position=(0, 64 * par))
                        firstmm = False
            return ins
        vres = []
        for p, (off, tp) in enumerate(pieces):
            ti = (ws + off) // 128
            vres += ["vth%d" % ti, "vtl%d" % ti]
        st("tensor", mmpv, r=vres + ["e32_%d" % pb, "el_%d" % pb], w=["pv%d" % bk_i],
           note=lbl + " PV p%d" % pr)
        st("vector", lambda v, _sz=sz, _bk=bk_i:
           v.tensor_tensor(out=pvt[:, 0:_sz], in0=pv[_bk][:, 0:_sz], in1=rb32[:, 0:_sz], op=ALU.mult),
           r=["pv%d" % bk_i, "rb32"], w=["pvt"], note=lbl + " pvm p%d" % pr)
        st("scalar", lambda s, _pr=pr, _sz=sz: s.copy(out=aeh[:, _pr * 128:_pr * 128 + _sz],
                                                      in_=pvt[:, 0:_sz]),
           r=["pvt"], w=["aeh"], note=lbl + " ah p%d" % pr)
        st("vector", lambda v, _pr=pr, _sz=sz:
           v.tensor_tensor(out=ael[:, _pr * 128:_pr * 128 + _sz], in0=pvt[:, 0:_sz],
                           in1=aeh[:, _pr * 128:_pr * 128 + _sz], op=ALU.subtract),
           r=["pvt", "aeh"], w=["ael"], note=lbl + " al p%d" % pr)

    def _par_off(par):
        return par * 384

    def xldh(q, hh):
        a, b = (0, HALF) if hh == 0 else (HALF, P)
        def f(s, _q=q, _a=a, _b=b):
            return s.dma_start(out=X.ap().rearrange("p (c t) -> p c t", c=NCH)[:, :, _a:_b],
                               in_=x_d[:, _q * OUT_T + _a:_q * OUT_T + _b].rearrange("(c p) t -> p c t", p=128))
        st("sync", f, w=["X%d" % hh], dma=True, note="q%d xld%d" % (q, hh))

    def ysth(q, hh):
        a, b = (LP, HALF) if hh == 0 else (HALF, LP + OUT_T)
        def f(s, _q=q, _a=a, _b=b):
            return s.dma_start(
                out=y_d[:, _q * OUT_T + _a - LP:_q * OUT_T + _b - LP].rearrange("(c p) t -> p c t", p=128),
                in_=X.ap().rearrange("p (c t) -> p c t", c=NCH)[:, :, _a:_b])
        st("sync", f, r=["X%d" % hh], dma=True, note="q%d yst%d" % (q, hh))

    def emit_cmask(q):
        st("vector", lambda v, _q=q: v.tensor_scalar(
            out=X.ap().rearrange("p (c t) -> p c t", c=NCH)[:, :, LP - 1:LP],
            in0=X.ap().rearrange("p (c t) -> p c t", c=NCH)[:, :, LP - 1:LP],
            scalar1=cms[:, _q:_q + 1], scalar2=None, op0=ALU.mult),
           r=["cms"], w=["X0"], note="q%d cmask" % q)

    def lnmix(hh, phase, want_hl=False):
        emit_ln_half(X, "X", hh, phase + " ln", want_hl=want_hl)
        if not want_hl:
            emit_mixes_half(hh, phase)

    if stop_after is not None:
        for g in phaseW_groups:
            replay(g)
        for q in range(NPASS):
            lbl = "q%d" % q
            xldh(q, 0)
            xldh(q, 1)
            f1 = lbl + " f1"
            lnmix(0, f1)
            lnmix(1, f1)
            if stop_after != "mix1":
                wr = load_use("wfr")
                ffn_proj_half(wr, 0, "r", f1)
                ffn_proj_half(wr, 1, "r", f1)
                wk = load_use("wfk")
                ffn_proj_half(wk, 0, "k", f1)
                ffn_proj_half(wk, 1, "k", f1)
                wv = load_use("wfv")
                ffn_proj_half(wv, 0, "v", f1)
                ffn_proj_half(wv, 1, "v", f1)
                if stop_after != "ffn1":
                    at = lbl + " at"
                    lnmix(0, at + " ln2", want_hl=True)
                    lnmix(1, at + " ln2", want_hl=True)
                    if stop_after != "ln2":
                        wak_u = load_use("wak")
                        attn_K_half(wak_u, 0, at)
                        attn_K_half(wak_u, 1, at)
                        wav_u = load_use("wav")
                        for ti in range(NBLK):
                            attn_V_tile(wav_u, ti, at)
                        qslot, qsres = load_use("wq")
                        oslot, osres = load_use("wo")
                        for bi0, nb2 in ((0, 2), (2, 2), (4, 1)):
                            attn_Q_pair(qslot, qsres, bi0, nb2, at)
                            for j in range(nb2):
                                attn_pairs(q, bi0 + j, 128 * j, qslot, qsres, at)
                                attn_wo(q, bi0 + j, oslot, osres, at)
            ysth(q, 0)
            ysth(q, 1)
    else:
        prev_kv = None
        for q in range(NPASS):
            lbl = "q%d" % q
            f1 = lbl + " f1"
            xldh(q, 0)
            ch0 = capture(lnmix, 0, f1)
            if prev_kv is not None:
                weave(ch0, prev_kv)
                ysth(q - 1, 1)
            else:
                weave(ch0, phaseW_groups, lead=8)
            xldh(q, 1)
            wr = load_use("wfr")
            wk = load_use("wfk")
            weave(capture(lnmix, 1, f1),
                  groups_of(ffn_proj_half, wr, 0, "r", f1) + groups_of(ffn_proj_half, wk, 0, "k", f1))
            ffn_proj_half(wr, 1, "r", f1)
            wv = load_use("wfv")
            ffn_proj_half(wv, 0, "v", f1)
            at = lbl + " at"
            weave(capture(lnmix, 0, at + " ln2", True),
                  groups_of(ffn_proj_half, wk, 1, "k", f1) + groups_of(ffn_proj_half, wv, 1, "v", f1))
            wak_u = load_use("wak")
            wav_u = load_use("wav")
            weave(capture(lnmix, 1, at + " ln2", True),
                  groups_of(attn_K_half, wak_u, 0, at)
                  + groups_of(attn_V_tile, wav_u, 0, at) + groups_of(attn_V_tile, wav_u, 1, at))
            attn_K_half(wak_u, 1, at)
            attn_V_tile(wav_u, 2, at)
            qslot, qsres = load_use("wq")
            attn_Q_pair(qslot, qsres, 0, 2, at)
            weave_even(capture(attn_pairs, q, 0, 0, qslot, qsres, at),
                       groups_of(attn_V_tile, wav_u, 3, at)
                       + groups_of(attn_V_tile, wav_u, 4, at))
            oslot, osres = load_use("wo")
            attn_pairs(q, 1, 128, qslot, qsres, at,
                       preempt=lambda: attn_wo(q, 0, oslot, osres, at))
            attn_Q_pair(qslot, qsres, 2, 2, at)
            attn_pairs(q, 2, 0, qslot, qsres, at,
                       preempt=lambda: attn_wo(q, 1, oslot, osres, at))
            f2 = lbl + " f2"
            def cm_ln3(_q=q):
                emit_cmask(_q)
                lnmix(0, f2)
            attn_wo(q, 2, oslot, osres, at)
            bgb = (groups_of(attn_pairs, q, 3, 128, qslot, qsres, at)
                   + groups_of(attn_Q_pair, qslot, qsres, 4, 1, at)
                   + groups_of(attn_pairs, q, 4, 0, qslot, qsres, at,
                               preempt=lambda: attn_wo(q, 3, oslot, osres, at))
                   + groups_of(attn_wo, q, 4, oslot, osres, at))
            weave(capture(cm_ln3), bgb, lead=12)
            wr = load_use("wfr")
            wk = load_use("wfk")
            weave(capture(lnmix, 1, f2),
                  groups_of(ffn_proj_half, wr, 0, "r", f2) + groups_of(ffn_proj_half, wk, 0, "k", f2))
            ffn_proj_half(wr, 1, "r", f2)
            wv = load_use("wfv")
            ffn_proj_half(wv, 0, "v", f2)
            ysth(q, 0)
            prev_kv = (groups_of(ffn_proj_half, wk, 1, "k", f2)
                       + groups_of(ffn_proj_half, wv, 1, "v", f2))
        for g in prev_kv:
            replay(g)
        ysth(NPASS - 1, 1)

    with nc.Block() as block:
        pr.emit(nc, block)
    return nc


def _stage_masks(h):
    out = np.full((NPASS * NBLK * MSLOT * 128, 128), NEGM, dtype=np.float32)
    for q in range(NPASS):
        g0 = 2048 * h + OUT_T * q - LP
        for bi, (r0, sz) in enumerate(BLOCKS):
            ws, pieces = blk_window(bi)
            i = g0 + r0 + np.arange(128)
            ivalid = (i >= 0) & (i < SEQ)
            chend = np.minimum((i // 10) * 10 + 10, SEQ)
            base = ((q * NBLK) + bi) * MSLOT * 128
            for p, (off, tp) in enumerate(pieces):
                j = g0 + ws + off + np.arange(128)
                jv = (j >= 0) & (j < SEQ) & (np.arange(128) < tp)
                allow = (ivalid[None, :] & jv[:, None]
                         & (j[:, None] >= i[None, :] - 25)
                         & (j[:, None] < chend[None, :]))
                blk = np.where(allow, 0.0, NEGM).astype(np.float32)
                out[base + p * 128: base + (p + 1) * 128, :] = blk
            colall = out[base:base + MSLOT * 128, :]
            dead = ~(colall > NEGM / 2).any(axis=0)
            out[base, dead] = 0.0
    return out.astype(ml_dtypes.bfloat16)


def kernel(L0_attn_hidden_states, L0_attn_Wq, L0_attn_bq, L0_attn_Wk, L0_attn_bk,
           L0_attn_Wv, L0_attn_bv, L0_attn_Wo, L0_attn_bo,
           L0_ffn_W_r, L0_ffn_W_k, L0_ffn_W_v, L0_ffn_mu_r, L0_ffn_mu_k):
    hid = np.asarray(L0_attn_hidden_states, dtype=np.float32)
    wmap = {"wfr": L0_ffn_W_r, "wfk": L0_ffn_W_k, "wfv": L0_ffn_W_v,
            "wq": L0_attn_Wq, "wak": L0_attn_Wk, "wav": L0_attn_Wv, "wo": L0_attn_Wo}
    wstaged = {k: np.ascontiguousarray(np.asarray(v, np.float32).T) for k, v in wmap.items()}
    vec = {"mur": L0_ffn_mu_r, "muk": L0_ffn_mu_k, "bq": L0_attn_bq, "bk": L0_attn_bk, "bo": L0_attn_bo}
    vstaged = {k: np.ascontiguousarray(np.asarray(v, np.float32).reshape(NCH, 128).T) for k, v in vec.items()}
    bv = np.ascontiguousarray(np.asarray(L0_attn_bv, np.float32).reshape(1, E))
    ident = np.eye(128, dtype=ml_dtypes.bfloat16)
    masks = {h: _stage_masks(h) for h in (0, 1)}
    cmask = {h: np.ones((128, NPASS), np.float32) for h in (0, 1)}
    cmask[0][:, 0] = 0.0

    in_maps = []
    for core in range(8):
        b, h = core // 2, core % 2
        xs = np.zeros((E, XCOLS), dtype=np.float32)
        s0 = 2048 * h - LP
        a, bnd = max(0, s0), min(SEQ, s0 + XCOLS)
        xs[:, a - s0:bnd - s0] = hid[b, a:bnd, :].T
        m = {"x": xs, "bv": bv, "ident": ident, "masks": masks[h], "cmask": cmask[h]}
        m.update(wstaged)
        m.update(vstaged)
        in_maps.append(m)

    nc = build_nc()
    res = run_bass_kernel_spmd(nc, in_maps, core_ids=list(range(8)))
    out = np.empty((NB, SEQ, E), dtype=np.float32)
    for core in range(8):
        b, h = core // 2, core % 2
        out[b, 2048 * h:2048 * (h + 1), :] = np.asarray(res.results[core]["y"]).T
    return out
